# version 02
# baseline (speedup 1.0000x reference)
import contextlib
import numpy as np
import ml_dtypes

import concourse.bass as bass
import concourse.mybir as mybir
from concourse.bass_utils import run_bass_kernel_spmd

F32 = mybir.dt.float32
BF16 = mybir.dt.bfloat16
I32 = mybir.dt.int32
AF = mybir.ActivationFunctionType
ALU = mybir.AluOpType
AX = mybir.AxisListType

D = 2048
S = 2048
NCORES = 8
HPC = 8
LAMBDA_INIT = 0.8
EPS = 1e-5
BF = ml_dtypes.bfloat16


def _tile_lhsT(w, km=128, mm=128):
    K, M = w.shape
    t = w.reshape(K // km, km, M // mm, mm).transpose(2, 1, 0, 3)
    return np.ascontiguousarray(t)


class Prog:
    ENGS = ("sync", "tensor", "vector", "scalar", "gpsimd")

    def __init__(self, nc, stack,
                 sems=("ld", "wld", "pe", "dve", "act", "pool", "sto", "aux")):
        self.nc = nc
        self.q = {e: [] for e in self.ENGS}
        self.ct = {}
        for n in sems:
            h = stack.enter_context(nc.semaphore(n))
            self.ct[n] = 0
            setattr(self, "s_" + n, h)

    def sem(self, name):
        return getattr(self, "s_" + name)

    def inc(self, name, n=1):
        self.ct[name] += n
        return self.ct[name]

    def add(self, eng, fn):
        self.q[eng].append(fn)

    def fence(self, src="dve", helper="scalar"):
        v = self.ct[src]
        a = self.inc("aux")
        ssem, asem = self.sem(src), self.s_aux
        hmap = {"scalar": "scalar", "gpsimd": "gpsimd", "vector": "vector",
                "sync": "sync"}
        eng = hmap[helper]
        self.add(eng, lambda e, v=v: (e.wait_ge(ssem, v), e.sem_inc(asem, 1)))
        srceng = {"dve": "vector", "act": "scalar", "pool": "gpsimd"}[src]
        self.add(srceng, lambda e, a=a: e.wait_ge(asem, a))

    def build(self, block):
        for name in self.ENGS:
            fns = self.q[name]

            def body(e, fns=fns):
                for fn in fns:
                    fn(e)
            getattr(block, name)(body)


def W(p, e, *pairs):
    for s, v in pairs:
        if v > 0:
            e.wait_ge(p.sem(s), v)



def build_launch_a(dbg=False):
    nc = bass.Bass("TRN2", target_bir_lowering=False, debug=False)
    xT = nc.dram_tensor("xT", [D, S], F32, kind="ExternalInput").ap()
    wq = nc.dram_tensor("wq", [8, 128, 16, 128], F32, kind="ExternalInput").ap()
    wk = nc.dram_tensor("wk", [8, 128, 16, 128], F32, kind="ExternalInput").ap()
    wv = nc.dram_tensor("wv", [8, 128, 16, 128], F32, kind="ExternalInput").ap()
    lamv = nc.dram_tensor("lamv", [1, 256], F32, kind="ExternalInput").ap()
    gnw = nc.dram_tensor("gnw", [128, 8], F32, kind="ExternalInput").ap()
    gnb = nc.dram_tensor("gnb", [128, 8], F32, kind="ExternalInput").ap()
    identin = nc.dram_tensor("identin", [128, 128], F32, kind="ExternalInput").ap()
    afm = nc.dram_tensor("afm", [HPC * 128, S], BF16, kind="ExternalOutput").ap()
    if dbg:
        d_ln = nc.dram_tensor("d_ln", [128, 16 * S], BF16, kind="ExternalOutput").ap()
        d_q = nc.dram_tensor("d_q", [128, 8 * S], BF16, kind="ExternalOutput").ap()
        d_k = nc.dram_tensor("d_k", [128, 8 * S], BF16, kind="ExternalOutput").ap()
        d_v = nc.dram_tensor("d_v", [128, 16 * 1024], BF16, kind="ExternalOutput").ap()
        d_lam = nc.dram_tensor("d_lam", [1, 160], F32, kind="ExternalOutput").ap()
        d_rows = nc.dram_tensor("d_rows", [1, 5 * S], F32, kind="ExternalOutput").ap()
        d_oc = nc.dram_tensor("d_oc", [128, S], F32, kind="ExternalOutput").ap()

    with contextlib.ExitStack() as stk, nc.Block() as block:
        p = Prog(nc, stk)
        sb = lambda name, shape, dt=F32, s=stk: s.enter_context(nc.sbuf_tensor(name, shape, dt))
        ps = lambda name, shape, dt=F32, s=stk: s.enter_context(nc.psum_tensor(name, shape, dt))

        qT = sb("qT", [128, 8 * S], BF16)
        kT = sb("kT", [128, 8 * S], BF16)
        V = sb("V", [128, 16 * 1024], BF16)
        kaug = sb("kaug", [65, S], BF16)
        qaug = sb("qaug", [65, S], BF16)
        slam = sb("slam", [1, 256])
        srow64 = sb("srow64", [1, 160])
        sgnw = sb("sgnw", [128, 8])
        sgnb = sb("sgnb", [128, 8])
        sident = sb("sident", [128, 128], BF16)
        sidf = sb("sidf", [128, 128], F32)
        onescol = sb("onescol", [128, 2])
        onescolb = sb("onescolb", [128, 2], BF16)
        onesrow = sb("onesrow", [1, 128])
        vtmp = sb("vtmp", [128, 512], BF16)


        ldv = p.inc("ld", 64)
        p.add("sync", lambda e, v=ldv: (
            e.dma_start(out=slam[:], in_=lamv).then_inc(p.s_ld, 16),
            e.dma_start(out=sgnw[:], in_=gnw).then_inc(p.s_ld, 16),
            e.dma_start(out=sgnb[:], in_=gnb).then_inc(p.s_ld, 16),
            e.dma_start(out=sidf[:], in_=identin).then_inc(p.s_ld, 16)))
        dv = p.inc("dve")
        p.add("vector", lambda e, v=ldv: (
            W(p, e, ("ld", v)),
            e.memset(onescol[:], 1.0),
            e.memset(onescolb[:], 1.0),
            e.memset(onesrow[:], 1.0),
            e.memset(kaug[64:65, :], 1.0),
            e.tensor_copy(out=sident[:], in_=sidf[:]),
            e.tensor_scalar_mul(out=sgnw[:], in0=sgnw[:], scalar1=1.0 - LAMBDA_INIT),
            e.tensor_scalar_mul(out=sgnb[:], in0=sgnb[:], scalar1=1.0 - LAMBDA_INIT),
            e.tensor_mul(out=srow64[0:1, 0:64], in0=slam[0:1, 0:64], in1=slam[0:1, 64:128]),
            e.tensor_mul(out=srow64[0:1, 64:128], in0=slam[0:1, 128:192], in1=slam[0:1, 192:256])
             .then_inc(p.s_dve, 1)))
        p.fence()
        dv = p.inc("dve")
        p.add("vector", lambda e: (
            e.reduce_sum(out=srow64[0:1, 128:129], in_=srow64[0:1, 0:64], axis=AX.X),
            e.reduce_sum(out=srow64[0:1, 132:133], in_=srow64[0:1, 64:128], axis=AX.X)
             .then_inc(p.s_dve, 1)))
        av = p.inc("act")
        p.add("scalar", lambda e, dv=dv, av=av: (
            W(p, e, ("dve", dv)),
            e.activation(out=srow64[0:1, 136:137], in_=srow64[0:1, 128:129], func=AF.Exp),
            e.activation(out=srow64[0:1, 140:141], in_=srow64[0:1, 132:133], func=AF.Exp)
             .then_inc(p.s_act, 1)))
        dv = p.inc("dve")
        p.add("vector", lambda e, av=av: (
            W(p, e, ("act", av)),
            e.scalar_tensor_tensor(out=srow64[0:1, 144:145], in0=srow64[0:1, 136:137],
                                   scalar=LAMBDA_INIT, in1=srow64[0:1, 140:141],
                                   op0=ALU.add, op1=ALU.subtract)
             .then_inc(p.s_dve, 1)))
        p.fence()

        with contextlib.ExitStack() as s1:
            lnT = sb("lnT", [128, 16 * S], BF16, s=s1)
            xchunk = sb("xchunk", [128, S], F32, s=s1)
            xsq = sb("xsq", [128, S], F32, s=s1)
            wbuff = sb("wbuff", [128, 16 * 128], F32, s=s1)
            wbufb = sb("wbufb", [128, 16 * 128], BF16, s=s1)
            LROW = {"mean": "xsq", "t0": "wbuff", "rstd": "wbuff", "t1": "xchunk"}

            def lrow(name, n=S, off=0):
                buf = {"xsq": xsq, "wbuff": wbuff, "xchunk": xchunk}[LROW[name]]
                return buf[0:1, off:off + n]

            with contextlib.ExitStack() as q1:
                p_su = ps("p_su", [1, S], s=q1)
                p_sq = ps("p_sq", [1, S], s=q1)
                for kk in range(16):
                    pev = p.ct["pe"]
                    ldv = p.inc("ld", 16)
                    p.add("sync", lambda e, kk=kk, pev=pev: (
                        W(p, e, ("pe", pev)),
                        e.dma_start(out=xchunk[:], in_=xT[kk * 128:(kk + 1) * 128, :])
                         .then_inc(p.s_ld, 16)))
                    av = p.inc("act")
                    p.add("scalar", lambda e, ldv=ldv, pev=pev: (
                        W(p, e, ("ld", ldv), ("pe", pev)),
                        e.activation(out=xsq[:], in_=xchunk[:], func=AF.Square)
                         .then_inc(p.s_act, 1)))
                    pv = p.inc("pe")
                    p.add("tensor", lambda e, kk=kk, av=av: (
                        W(p, e, ("act", av)),
                        [e.matmul(p_su[0:1, tt * 512:(tt + 1) * 512], onescol[:, 0:1],
                                  xchunk[:, tt * 512:(tt + 1) * 512],
                                  start=(kk == 0), stop=(kk == 15)) for tt in range(4)],
                        [e.matmul(p_sq[0:1, tt * 512:(tt + 1) * 512], onescol[:, 0:1],
                                  xsq[:, tt * 512:(tt + 1) * 512],
                                  start=(kk == 0), stop=(kk == 15)) for tt in range(4)][-1]
                        .then_inc(p.s_pe, 1)))

                pev = p.ct["pe"]
                dv = p.inc("dve")
                p.add("vector", lambda e, pev=pev: (
                    W(p, e, ("pe", pev)),
                    e.tensor_scalar_mul(out=lrow("mean"), in0=p_su[0:1, :], scalar1=1.0 / D),
                    e.tensor_scalar_mul(out=lrow("t0"), in0=p_sq[0:1, :], scalar1=1.0 / D),
                    e.tensor_mul(out=lrow("t1"), in0=lrow("mean"), in1=lrow("mean")),
                    e.tensor_sub(out=lrow("t0"), in0=lrow("t0"), in1=lrow("t1")),
                    e.tensor_scalar_add(out=lrow("t0"), in0=lrow("t0"), scalar1=EPS)
                     .then_inc(p.s_dve, 1)))
                av = p.inc("act")
                p.add("scalar", lambda e, dv=dv: (
                    W(p, e, ("dve", dv)),
                    e.activation(out=lrow("t1"), in_=lrow("t0"), func=AF.Sqrt)
                     .then_inc(p.s_act, 1)))
                dv = p.inc("dve")
                p.add("vector", lambda e, av=av: (
                    W(p, e, ("act", av)),
                    e.reciprocal(out=lrow("rstd"), in_=lrow("t1")).then_inc(p.s_dve, 1)))
                p.fence()
                stats_dve = p.ct["dve"]

            with contextlib.ExitStack() as q2:
                p_mb = ps("p_mb", [128, S], s=q2)
                p_rb = ps("p_rb", [128, S], s=q2)
                bcast_pe = p.inc("pe")
                p.add("tensor", lambda e, dv=stats_dve: (
                    W(p, e, ("dve", dv)),
                    [e.matmul(p_mb[:, tt * 512:(tt + 1) * 512], onesrow[:],
                              lrow("mean", 512, tt * 512), start=True, stop=True)
                     for tt in range(4)],
                    [e.matmul(p_rb[:, tt * 512:(tt + 1) * 512], onesrow[:],
                              lrow("rstd", 512, tt * 512), start=True, stop=True)
                     for tt in range(4)][-1].then_inc(p.s_pe, 1)))

                for kk in range(16):
                    dvp = p.ct["dve"]
                    ldv = p.inc("ld", 16)
                    p.add("sync", lambda e, kk=kk, dvp=dvp, bp=bcast_pe: (
                        W(p, e, ("dve", dvp), ("pe", bp)),
                        e.dma_start(out=xchunk[:], in_=xT[kk * 128:(kk + 1) * 128, :])
                         .then_inc(p.s_ld, 16)))
                    dv = p.inc("dve")
                    p.add("vector", lambda e, kk=kk, ldv=ldv, bp=bcast_pe: (
                        W(p, e, ("ld", ldv), ("pe", bp)),
                        e.tensor_sub(out=xsq[:], in0=xchunk[:], in1=p_mb[:]),
                        e.tensor_mul(out=lnT[:, kk * S:(kk + 1) * S], in0=xsq[:],
                                     in1=p_rb[:]).then_inc(p.s_dve, 1)))
                ln_dve = p.ct["dve"]

            if dbg:
                dvv = p.ct["dve"]
                stv = p.inc("sto", 16)
                p.add("gpsimd", lambda e, dvv=dvv: (
                    W(p, e, ("dve", dvv)),
                    e.dma_start(out=d_ln, in_=lnT[:]).then_inc(p.s_sto, 16)))

            with contextlib.ExitStack() as q4:
                p_mm = ps("p_mm", [128, S], s=q4)
                p_tp = ps("p_tp", [128, 128], BF16, s=q4)

                def proj(wsrc, m):
                    poolv = p.ct["pool"]
                    pev = p.ct["pe"]
                    wldv = p.inc("wld", 16)
                    p.add("sync", lambda e, wsrc=wsrc, m=m, poolv=poolv: (
                        W(p, e, ("pool", poolv)),
                        e.dma_start(out=wbuff[:], in_=wsrc[m]).then_inc(p.s_wld, 16)))
                    plv = p.inc("pool")
                    p.add("gpsimd", lambda e, wldv=wldv, pev=pev: (
                        W(p, e, ("wld", wldv), ("pe", pev)),
                        e.tensor_copy(out=wbufb[:], in_=wbuff[:]).then_inc(p.s_pool, 1)))
                    dvv = p.ct["dve"]
                    pv = p.inc("pe")
                    p.add("tensor", lambda e, plv=plv, dvv=dvv, lv=ln_dve: (
                        W(p, e, ("pool", plv), ("dve", dvv), ("dve", lv)),
                        [e.matmul(p_mm[:, tt * 512:(tt + 1) * 512],
                                  wbufb[:, kk * 128:(kk + 1) * 128],
                                  lnT[:, kk * S + tt * 512: kk * S + tt * 512 + 512],
                                  start=(kk == 0), stop=(kk == 15))
                         for tt in range(4) for kk in range(16)][-1]
                        .then_inc(p.s_pe, 1)))
                    return pv

                for wsrc, dest in ((wq, qT), (wk, kT)):
                    for m in range(8):
                        pv = proj(wsrc, m)
                        p.inc("dve")
                        p.add("vector", lambda e, pv=pv, dest=dest, m=m: (
                            W(p, e, ("pe", pv)),
                            e.tensor_copy(out=dest[:, m * S:(m + 1) * S], in_=p_mm[:])
                             .then_inc(p.s_dve, 1)))

                for m in range(8):
                    pv = proj(wv, m)
                    for tt in range(4):
                        pet = p.ct["pe"]
                        p.inc("dve")
                        p.add("vector", lambda e, pv=pv, pet=pet, tt=tt: (
                            W(p, e, ("pe", max(pv, pet))),
                            e.tensor_copy(out=vtmp[:],
                                          in_=p_mm[:, tt * 512:(tt + 1) * 512])
                             .then_inc(p.s_dve, 1)))
                        for j in range(4):
                            dvw = p.ct["dve"]
                            dvp = p.ct["dve"]
                            tppe = p.inc("pe")
                            p.add("tensor", lambda e, dvw=dvw, j=j: (
                                W(p, e, ("dve", dvw)),
                                e.transpose(p_tp[:], vtmp[:, j * 128:(j + 1) * 128],
                                            sident[:]).then_inc(p.s_pe, 1)))
                            p.inc("dve")
                            p.add("vector", lambda e, tppe=tppe, tt=tt, j=j, m=m: (
                                W(p, e, ("pe", tppe)),
                                e.tensor_copy(
                                    out=V[:, (tt * 4 + j) * 1024 + m * 128:
                                           (tt * 4 + j) * 1024 + (m + 1) * 128],
                                    in_=p_tp[:]).then_inc(p.s_dve, 1)))

        if dbg:
            dvv = p.ct["dve"]
            stv = p.inc("sto", 48)
            p.add("gpsimd", lambda e, dvv=dvv: (
                W(p, e, ("dve", dvv)),
                e.dma_start(out=d_q, in_=qT[:]).then_inc(p.s_sto, 16),
                e.dma_start(out=d_k, in_=kT[:]).then_inc(p.s_sto, 16),
                e.dma_start(out=d_v, in_=V[:]).then_inc(p.s_sto, 16)))

        with contextlib.ExitStack() as s2:
            pT = sb("pT", [128, 16 * 1024], BF16, s=s2)
            out_c = sb("out_c", [128, S], s=s2)
            gsc = sb("gsc", [128, S], s=s2)
            tmps = sb("tmps", [128, 512], s=s2)
            afm_t = sb("afm_t", [128, S], BF16, s=s2)
            arows = sb("arows", [1, 5 * S], s=s2)
            ARN = {"r1": 0, "gmean": 1, "grstd": 2, "t0": 3, "t1": 4}

            def arow(name, n=S, off=0):
                return arows[0:1, ARN[name] * S + off: ARN[name] * S + off + n]

            with contextlib.ExitStack() as q5:
                p_sc = ps("p_sc", [128, 2048], s=q5)
                p_o1 = ps("p_o1", [128, 512], s=q5)
                p_o2 = ps("p_o2", [128, 512], s=q5)
                p_r = ps("p_r", [1, 512], s=q5)
                p_b = ps("p_b", [128, 512], s=q5)

                def rowslot(i, n=512):
                    return p_sc[0:1, i * 512: i * 512 + n]

                for h in range(8):
                    for g in range(2):
                        pb = 64 * (h % 2)
                        m = g * 4 + h // 2

                        pev = p.ct["pe"]
                        dvv = p.ct["dve"]
                        av = p.inc("act")
                        p.add("scalar", lambda e, pev=pev, dvv=dvv, pb=pb, m=m: (
                            W(p, e, ("pe", pev), ("dve", dvv)),
                            e.activation(out=gsc[pb:pb + 64, 0:S],
                                         in_=qT[pb:pb + 64, m * S:(m + 1) * S],
                                         func=AF.Square).then_inc(p.s_act, 1)))
                        pv = p.inc("pe")
                        p.add("tensor", lambda e, av=av, pb=pb: (
                            W(p, e, ("act", av)),
                            [e.matmul(rowslot(tt), onescol[pb:pb + 64, 0:1],
                                      gsc[pb:pb + 64, tt * 512:(tt + 1) * 512],
                                      start=True, stop=True) for tt in range(4)][-1]
                            .then_inc(p.s_pe, 1)))
                        dv = p.inc("dve")
                        p.add("vector", lambda e, pv=pv: (
                            W(p, e, ("pe", pv)),
                            e.tensor_copy(out=arow("t0"), in_=p_sc[0:1, 0:S])
                             .then_inc(p.s_dve, 1)))
                        av2 = p.inc("act")
                        p.add("scalar", lambda e, dv=dv, pb=pb, m=m: (
                            W(p, e, ("dve", dv)),
                            e.activation(out=gsc[pb:pb + 64, 0:S],
                                         in_=kT[pb:pb + 64, m * S:(m + 1) * S],
                                         func=AF.Square).then_inc(p.s_act, 1)))
                        pv2 = p.inc("pe")
                        p.add("tensor", lambda e, av2=av2, pb=pb: (
                            W(p, e, ("act", av2)),
                            [e.matmul(rowslot(tt), onescol[pb:pb + 64, 0:1],
                                      gsc[pb:pb + 64, tt * 512:(tt + 1) * 512],
                                      start=True, stop=True) for tt in range(4)][-1]
                            .then_inc(p.s_pe, 1)))
                        dv2 = p.inc("dve")
                        p.add("vector", lambda e, pv2=pv2: (
                            W(p, e, ("pe", pv2)),
                            e.reduce_max(out=arow("t1", 1), in_=p_sc[0:1, 0:S],
                                         axis=AX.X).then_inc(p.s_dve, 1)))
                        av3 = p.inc("act")
                        p.add("scalar", lambda e, dv2=dv2: (
                            W(p, e, ("dve", dv2)),
                            e.activation(out=arow("t0"), in_=arow("t0"), func=AF.Sqrt,
                                         scale=arow("t1", 1)).then_inc(p.s_act, 1)))
                        bdv = p.inc("dve")
                        p.add("vector", lambda e, av3=av3: (
                            W(p, e, ("act", av3)),
                            e.tensor_scalar_mul(out=qaug[64:65, :], in0=arow("t0"),
                                                scalar1=-1.0).then_inc(p.s_dve, 1)))

                        pev = p.ct["pe"]
                        augv = p.inc("ld", 32)
                        p.add("scalar", lambda e, pev=pev, pb=pb, m=m: (
                            W(p, e, ("pe", pev)),
                            e.dma_start(out=kaug[0:64, :],
                                        in_=kT[pb:pb + 64, m * S:(m + 1) * S])
                             .then_inc(p.s_ld, 16),
                            e.dma_start(out=qaug[0:64, :],
                                        in_=qT[pb:pb + 64, m * S:(m + 1) * S])
                             .then_inc(p.s_ld, 16)))

                        for qh in range(2):
                            exp_av = []
                            for kt in range(16):
                                half = kt % 2
                                wait_av = exp_av[kt - 2] if kt >= 2 else 0
                                pv = p.inc("pe")
                                p.add("tensor", lambda e, kt=kt, half=half, qh=qh,
                                      augv=augv, bdv=bdv, wav=wait_av: (
                                    W(p, e, ("ld", augv), ("dve", bdv), ("act", wav)),
                                    [e.matmul(p_sc[:, half * 1024 + qn * 512:
                                                   half * 1024 + qn * 512 + 512],
                                              kaug[:, kt * 128:(kt + 1) * 128],
                                              qaug[:, qh * 1024 + qn * 512:
                                                   qh * 1024 + qn * 512 + 512],
                                              start=True, stop=True)
                                     for qn in range(2)][-1].then_inc(p.s_pe, 1)))
                                av = p.inc("act")
                                exp_av.append(av)
                                p.add("scalar", lambda e, pv=pv, kt=kt, half=half: (
                                    W(p, e, ("pe", pv)),
                                    e.activation(out=pT[:, kt * 1024:(kt + 1) * 1024],
                                                 in_=p_sc[:, half * 1024:
                                                          half * 1024 + 1024],
                                                 func=AF.Exp, scale=0.125)
                                     .then_inc(p.s_act, 1)))
                            all_exp = p.ct["act"]

                            for qn in range(2):
                                qabs = qh * 2 + qn
                                dvo = p.ct["dve"]
                                pv = p.inc("pe")
                                p.add("tensor", lambda e, ae=all_exp, qn=qn, g=g,
                                      h=h, dvo=dvo: (
                                    W(p, e, ("act", ae), ("dve", dvo)),
                                    [e.matmul((p_o1 if g == 0 else p_o2)[:, :],
                                              V[:, kt2 * 1024 + h * 128:
                                                 kt2 * 1024 + (h + 1) * 128],
                                              pT[:, kt2 * 1024 + qn * 512:
                                                 kt2 * 1024 + qn * 512 + 512],
                                              start=(kt2 == 0), stop=(kt2 == 15))
                                     for kt2 in range(16)],
                                    [e.matmul(p_r[0:1, :], onescolb[:, 0:1],
                                              pT[:, kt2 * 1024 + qn * 512:
                                                 kt2 * 1024 + qn * 512 + 512],
                                              start=(kt2 == 0), stop=(kt2 == 15))
                                     for kt2 in range(16)][-1].then_inc(p.s_pe, 1)))
                                if g == 0:
                                    dv = p.inc("dve")
                                    p.add("vector", lambda e, pv=pv, qabs=qabs: (
                                        W(p, e, ("pe", pv)),
                                        e.tensor_copy(out=out_c[:, qabs * 512:
                                                          (qabs + 1) * 512],
                                                      in_=p_o1[:]),
                                        e.tensor_copy(out=arow("r1", 512, qabs * 512),
                                                      in_=p_r[0:1, :])
                                         .then_inc(p.s_dve, 1)))
                                else:
                                    dv = p.inc("dve")
                                    p.add("vector", lambda e, pv=pv, qabs=qabs: (
                                        W(p, e, ("pe", pv)),
                                        e.reciprocal(out=arow("t0", 512),
                                                     in_=p_r[0:1, :])
                                         .then_inc(p.s_dve, 1)))
                                    p.fence()
                                    dv = p.inc("dve")
                                    p.add("vector", lambda e, qabs=qabs: (
                                        e.tensor_mul(out=arow("t1", 512),
                                                     in0=arow("t0", 512),
                                                     in1=arow("r1", 512, qabs * 512)),
                                        e.tensor_scalar_mul(
                                            out=arow("t1", 512),
                                            in0=arow("t1", 512),
                                            scalar1=srow64[0:1, 144:145])
                                         .then_inc(p.s_dve, 1)))
                                    pvb = p.inc("pe")
                                    p.add("tensor", lambda e, dv=dv, qabs=qabs: (
                                        W(p, e, ("dve", dv)),
                                        e.matmul(p_b[:], onesrow[:],
                                                 arow("t1", 512),
                                                 start=True, stop=True)
                                         .then_inc(p.s_pe, 1)))
                                    dv2 = p.inc("dve")
                                    p.add("vector", lambda e, pvb=pvb, qabs=qabs: (
                                        W(p, e, ("pe", pvb)),
                                        e.tensor_copy(out=tmps[:], in_=p_b[:]),
                                        e.tensor_mul(out=tmps[:], in0=p_o2[:],
                                                     in1=tmps[:]),
                                        e.tensor_sub(out=out_c[:, qabs * 512:
                                                         (qabs + 1) * 512],
                                                     in0=out_c[:, qabs * 512:
                                                              (qabs + 1) * 512],
                                                     in1=tmps[:])
                                         .then_inc(p.s_dve, 1)))

                    if dbg and h == 7:
                        dvv = p.ct["dve"]
                        p.inc("sto", 48)
                        p.add("gpsimd", lambda e, dvv=dvv: (
                            W(p, e, ("dve", dvv)),
                            e.dma_start(out=d_lam, in_=srow64[0:1, :]).then_inc(p.s_sto, 16),
                            e.dma_start(out=d_rows, in_=arows[0:1, :]).then_inc(p.s_sto, 16),
                            e.dma_start(out=d_oc, in_=out_c[:]).then_inc(p.s_sto, 16)))

                    cdv = p.ct["dve"]
                    pv = p.inc("pe")
                    p.add("tensor", lambda e, cdv=cdv: (
                        W(p, e, ("dve", cdv)),
                        [e.matmul(rowslot(tt), onescol[:, 0:1],
                                  out_c[:, tt * 512:(tt + 1) * 512],
                                  start=True, stop=True) for tt in range(4)][-1]
                        .then_inc(p.s_pe, 1)))
                    dv = p.inc("dve")
                    p.add("vector", lambda e, pv=pv: (
                        W(p, e, ("pe", pv)),
                        e.tensor_scalar_mul(out=arow("gmean"), in0=p_sc[0:1, 0:S],
                                            scalar1=1.0 / 128.0).then_inc(p.s_dve, 1)))
                    av = p.inc("act")
                    p.add("scalar", lambda e, cdv=cdv: (
                        W(p, e, ("dve", cdv)),
                        e.activation(out=gsc[:], in_=out_c[:], func=AF.Square)
                         .then_inc(p.s_act, 1)))
                    pv2 = p.inc("pe")
                    p.add("tensor", lambda e, av=av, dv=dv: (
                        W(p, e, ("act", av), ("dve", dv)),
                        [e.matmul(rowslot(tt), onescol[:, 0:1],
                                  gsc[:, tt * 512:(tt + 1) * 512],
                                  start=True, stop=True) for tt in range(4)][-1]
                        .then_inc(p.s_pe, 1)))
                    dv2 = p.inc("dve")
                    p.add("vector", lambda e, pv2=pv2: (
                        W(p, e, ("pe", pv2)),
                        e.tensor_scalar_mul(out=arow("t0"), in0=p_sc[0:1, 0:S],
                                            scalar1=1.0 / 128.0),
                        e.tensor_mul(out=arow("t1"), in0=arow("gmean"), in1=arow("gmean")),
                        e.tensor_sub(out=arow("t0"), in0=arow("t0"), in1=arow("t1")),
                        e.tensor_scalar_add(out=arow("t0"), in0=arow("t0"), scalar1=EPS)
                         .then_inc(p.s_dve, 1)))
                    av2 = p.inc("act")
                    p.add("scalar", lambda e, dv2=dv2: (
                        W(p, e, ("dve", dv2)),
                        e.activation(out=arow("t1"), in_=arow("t0"), func=AF.Sqrt)
                         .then_inc(p.s_act, 1)))
                    dv3 = p.inc("dve")
                    p.add("vector", lambda e, av2=av2: (
                        W(p, e, ("act", av2)),
                        e.reciprocal(out=arow("grstd"), in_=arow("t1"))
                         .then_inc(p.s_dve, 1)))
                    p.fence()
                    gst_dve = p.ct["dve"]

                    for qn in range(4):
                        pvm = p.inc("pe")
                        p.add("tensor", lambda e, dv=gst_dve, qn=qn,
                              dvw=p.ct["dve"]: (
                            W(p, e, ("dve", max(dv, dvw))),
                            e.matmul(p_b[:], onesrow[:], arow("gmean", 512, qn * 512),
                                     start=True, stop=True).then_inc(p.s_pe, 1)))
                        dv = p.inc("dve")
                        p.add("vector", lambda e, pvm=pvm, qn=qn: (
                            W(p, e, ("pe", pvm)),
                            e.tensor_sub(out=gsc[:, qn * 512:(qn + 1) * 512],
                                         in0=out_c[:, qn * 512:(qn + 1) * 512],
                                         in1=p_b[:]).then_inc(p.s_dve, 1)))
                        pvr = p.inc("pe")
                        p.add("tensor", lambda e, dv=dv, qn=qn: (
                            W(p, e, ("dve", dv)),
                            e.matmul(p_b[:], onesrow[:], arow("grstd", 512, qn * 512),
                                     start=True, stop=True).then_inc(p.s_pe, 1)))
                        stv = p.ct["sto"]
                        dv2 = p.inc("dve")
                        p.add("vector", lambda e, pvr=pvr, qn=qn, h=h, stv=stv: (
                            W(p, e, ("pe", pvr), ("sto", stv)),
                            e.tensor_mul(out=gsc[:, qn * 512:(qn + 1) * 512],
                                         in0=gsc[:, qn * 512:(qn + 1) * 512],
                                         in1=p_b[:]),
                            e.tensor_scalar(out=afm_t[:, qn * 512:(qn + 1) * 512],
                                            in0=gsc[:, qn * 512:(qn + 1) * 512],
                                            scalar1=sgnw[:, h:h + 1],
                                            scalar2=sgnb[:, h:h + 1],
                                            op0=ALU.mult, op1=ALU.add)
                             .then_inc(p.s_dve, 1)))

                    hdv = p.ct["dve"]
                    stv = p.inc("sto", 16)
                    p.add("gpsimd", lambda e, hdv=hdv, h=h: (
                        W(p, e, ("dve", hdv)),
                        e.dma_start(out=afm[h * 128:(h + 1) * 128, :], in_=afm_t[:])
                         .then_inc(p.s_sto, 16)))

        p.add("sync", lambda e, v=p.ct["sto"]: W(p, e, ("sto", v)))

        p.build(block)
    return nc



def stage_a(inputs, core):
    s, half = core // 2, core % 2
    x = np.asarray(inputs["L0_attn_x"])[s]
    Wq = np.asarray(inputs["L0_attn_Wq"])
    Wk = np.asarray(inputs["L0_attn_Wk"])
    Wv = np.asarray(inputs["L0_attn_Wv"])
    gw = np.asarray(inputs["L0_attn_gn_weight"])
    gb = np.asarray(inputs["L0_attn_gn_bias"])
    lam = np.concatenate([np.asarray(inputs["L0_attn_lambda_q1"]),
                          np.asarray(inputs["L0_attn_lambda_k1"]),
                          np.asarray(inputs["L0_attn_lambda_q2"]),
                          np.asarray(inputs["L0_attn_lambda_k2"])])[None, :]

    cols = np.arange(1024 * half, 1024 * half + 1024)
    hh, gg, ii = np.meshgrid(np.arange(8), np.arange(2), np.arange(64),
                             indexing="ij")
    perm = (hh * 128 + gg * 64 + ii).transpose(1, 0, 2).reshape(-1)
    qcols = cols[perm]

    def tile(w, c):
        return _tile_lhsT(np.ascontiguousarray(w[:, c]))

    return dict(
        xT=np.ascontiguousarray(x.T),
        wq=tile(Wq, qcols), wk=tile(Wk, qcols), wv=tile(Wv, cols),
        lamv=np.ascontiguousarray(lam.astype(np.float32)),
        gnw=np.ascontiguousarray(gw[cols].reshape(8, 128).T),
        gnb=np.ascontiguousarray(gb[cols].reshape(8, 128).T),
        identin=np.eye(128, dtype=np.float32),
    )




def build_launch_b(dbg=False):
    nc = bass.Bass("TRN2", target_bir_lowering=False, debug=False)
    T = 1024
    xT = nc.dram_tensor("xT", [D, T], F32, kind="ExternalInput").ap()
    ofm = nc.dram_tensor("ofm", [D, T], BF16, kind="ExternalInput").ap()
    wo = nc.dram_tensor("wo", [16, 128, 16, 128], F32, kind="ExternalInput").ap()
    ws1 = nc.dram_tensor("ws1", [11, 128, 16, 128], F32, kind="ExternalInput").ap()
    ws2 = nc.dram_tensor("ws2", [16, 128, 11, 128], F32, kind="ExternalInput").ap()
    w1 = nc.dram_tensor("w1", [8, 11, 128, 16, 128], F32, kind="ExternalInput").ap()
    w2 = nc.dram_tensor("w2", [8, 16, 128, 11, 128], F32, kind="ExternalInput").ap()
    wr = nc.dram_tensor("wr", [16, 128, 8], F32, kind="ExternalInput").ap()
    identin = nc.dram_tensor("identin", [128, 128], F32, kind="ExternalInput").ap()
    xo = nc.dram_tensor("xo", [D, T], F32, kind="ExternalOutput").ap()
    if dbg:
        d_xwo = nc.dram_tensor("d_xwo", [128, 16 * T], F32, kind="ExternalOutput").ap()
        d_ln = nc.dram_tensor("d_ln", [128, 16 * T], BF16, kind="ExternalOutput").ap()
        d_rl = nc.dram_tensor("d_rl", [8, T], F32, kind="ExternalOutput").ap()
        d_comb = nc.dram_tensor("d_comb", [8, T], F32, kind="ExternalOutput").ap()
        d_mid = nc.dram_tensor("d_mid", [128, 11 * T], BF16, kind="ExternalOutput").ap()

    with contextlib.ExitStack() as stk, nc.Block() as block:
        p = Prog(nc, stk)
        sb = lambda name, shape, dt=F32, s=stk: s.enter_context(nc.sbuf_tensor(name, shape, dt))
        ps = lambda name, shape, dt=F32, s=stk: s.enter_context(nc.psum_tensor(name, shape, dt))

        x = sb("x", [128, 16 * T])
        lnT = sb("lnT", [128, 16 * T], BF16)
        wbuff = sb("wbuff", [128, 16 * 128], F32)
        wbufb = sb("wbufb", [128, 16 * 128], BF16)
        mid = sb("mid", [128, 11 * T], BF16)
        sqs = sb("sqs", [128, T])
        cbs = sb("cbs", [128, T])
        probs = sb("probs", [128, 8 * 8])
        combT = sb("combT", [8, 8 * 128])
        combE = sb("combE", [1, 1024])
        rl = sb("rl", [8, T])
        sident = sb("sident", [128, 128], BF16)
        sidf = sb("sidf", [128, 128], F32)
        onescol = sb("onescol", [128, 2])
        onesrow = sb("onesrow", [1, 128])
        st1 = sb("st1", [128, 512], BF16)
        colsc = sb("colsc", [128, 8])

        ldv = p.inc("ld", 16)
        p.add("sync", lambda e: (
            e.dma_start(out=sidf[:], in_=identin).then_inc(p.s_ld, 16)))
        dv = p.inc("dve")
        p.add("vector", lambda e, v=ldv: (
            W(p, e, ("ld", v)),
            e.memset(onescol[:], 1.0),
            e.memset(onesrow[:], 1.0),
            e.tensor_copy(out=sident[:], in_=sidf[:]).then_inc(p.s_dve, 1)))

        with contextlib.ExitStack() as s0:
            ofms = sb("ofms", [128, 16 * T], BF16, s=s0)
            xldv = p.inc("ld", 16 * 2)
            p.add("sync", lambda e: (
                e.dma_start(out=x[:].rearrange("p (k t) -> p k t", t=T),
                            in_=xT.rearrange("(k p) t -> p k t", p=128))
                 .then_inc(p.s_ld, 16),
                e.dma_start(out=ofms[:].rearrange("p (k t) -> p k t", t=T),
                            in_=ofm.rearrange("(k p) t -> p k t", p=128))
                 .then_inc(p.s_ld, 16)))

            with contextlib.ExitStack() as q0:
                p_mm = ps("p_mm", [128, T], s=q0)
                for m in range(16):
                    poolv = p.ct["pool"]
                    pev = p.ct["pe"]
                    wldv = p.inc("wld", 16)
                    p.add("sync", lambda e, m=m, poolv=poolv: (
                        W(p, e, ("pool", poolv)),
                        e.dma_start(out=wbuff[:], in_=wo[m]).then_inc(p.s_wld, 16)))
                    plv = p.inc("pool")
                    p.add("gpsimd", lambda e, wldv=wldv, pev=pev: (
                        W(p, e, ("wld", wldv), ("pe", pev)),
                        e.tensor_copy(out=wbufb[:], in_=wbuff[:]).then_inc(p.s_pool, 1)))
                    dvv = p.ct["dve"]
                    pv = p.inc("pe")
                    p.add("tensor", lambda e, plv=plv, dvv=dvv, xv=xldv: (
                        W(p, e, ("pool", plv), ("dve", dvv), ("ld", xv)),
                        [e.matmul(p_mm[:, tt * 512:(tt + 1) * 512],
                                  wbufb[:, kk * 128:(kk + 1) * 128],
                                  ofms[:, kk * T + tt * 512: kk * T + tt * 512 + 512],
                                  start=(kk == 0), stop=(kk == 15))
                         for tt in range(2) for kk in range(16)][-1]
                        .then_inc(p.s_pe, 1)))
                    dv = p.inc("dve")
                    p.add("vector", lambda e, pv=pv, m=m: (
                        W(p, e, ("pe", pv)),
                        e.tensor_add(out=x[:, m * T:(m + 1) * T],
                                     in0=x[:, m * T:(m + 1) * T], in1=p_mm[:])
                         .then_inc(p.s_dve, 1)))
        wo_done_dve = p.ct["dve"]
        if dbg:
            p.inc("sto", 16)
            p.add("gpsimd", lambda e, v=wo_done_dve: (
                W(p, e, ("dve", v)),
                e.dma_start(out=d_xwo, in_=x[:]).then_inc(p.s_sto, 16)))

        lrows = sb("lrows", [1, 4 * T])

        def lrow(name, n=T, off=0):
            i = {"mean": 0, "t0": 1, "t1": 2, "rstd": 3}[name]
            return lrows[0:1, i * T + off: i * T + off + n]

        with contextlib.ExitStack() as q1:
            p_su = ps("p_su", [1, T], s=q1)
            p_sq = ps("p_sq", [1, T], s=q1)
            for kk in range(16):
                pev = p.ct["pe"]
                av = p.inc("act")
                p.add("scalar", lambda e, kk=kk, wv=wo_done_dve, pev=pev: (
                    W(p, e, ("dve", wv), ("pe", pev)),
                    e.activation(out=sqs[:], in_=x[:, kk * T:(kk + 1) * T],
                                 func=AF.Square).then_inc(p.s_act, 1)))
                pv = p.inc("pe")
                p.add("tensor", lambda e, kk=kk, av=av: (
                    W(p, e, ("act", av)),
                    [e.matmul(p_su[0:1, tt * 512:(tt + 1) * 512], onescol[:, 0:1],
                              x[:, kk * T + tt * 512: kk * T + tt * 512 + 512],
                              start=(kk == 0), stop=(kk == 15)) for tt in range(2)],
                    [e.matmul(p_sq[0:1, tt * 512:(tt + 1) * 512], onescol[:, 0:1],
                              sqs[:, tt * 512:(tt + 1) * 512],
                              start=(kk == 0), stop=(kk == 15)) for tt in range(2)][-1]
                    .then_inc(p.s_pe, 1)))
            pev = p.ct["pe"]
            dv = p.inc("dve")
            p.add("vector", lambda e, pev=pev: (
                W(p, e, ("pe", pev)),
                e.tensor_scalar_mul(out=lrow("mean"), in0=p_su[0:1, :], scalar1=1.0 / D),
                e.tensor_scalar_mul(out=lrow("t0"), in0=p_sq[0:1, :], scalar1=1.0 / D),
                e.tensor_mul(out=lrow("t1"), in0=lrow("mean"), in1=lrow("mean")),
                e.tensor_sub(out=lrow("t0"), in0=lrow("t0"), in1=lrow("t1")),
                e.tensor_scalar_add(out=lrow("t0"), in0=lrow("t0"), scalar1=EPS)
                 .then_inc(p.s_dve, 1)))
            av = p.inc("act")
            p.add("scalar", lambda e, dv=dv: (
                W(p, e, ("dve", dv)),
                e.activation(out=lrow("t1"), in_=lrow("t0"), func=AF.Sqrt)
                 .then_inc(p.s_act, 1)))
            dv = p.inc("dve")
            p.add("vector", lambda e, av=av: (
                W(p, e, ("act", av)),
                e.reciprocal(out=lrow("rstd"), in_=lrow("t1")).then_inc(p.s_dve, 1)))
            p.fence()

        with contextlib.ExitStack() as q2:
            p_mb = ps("p_mb", [128, T], s=q2)
            p_rb = ps("p_rb", [128, T], s=q2)
            sdv = p.ct["dve"]
            bpe = p.inc("pe")
            p.add("tensor", lambda e, sdv=sdv: (
                W(p, e, ("dve", sdv)),
                [e.matmul(p_mb[:, tt * 512:(tt + 1) * 512], onesrow[:],
                          lrow("mean", 512, tt * 512), start=True, stop=True)
                 for tt in range(2)],
                [e.matmul(p_rb[:, tt * 512:(tt + 1) * 512], onesrow[:],
                          lrow("rstd", 512, tt * 512), start=True, stop=True)
                 for tt in range(2)][-1].then_inc(p.s_pe, 1)))
            for kk in range(16):
                dv = p.inc("dve")
                p.add("vector", lambda e, kk=kk, bpe=bpe: (
                    W(p, e, ("pe", bpe)),
                    e.tensor_sub(out=sqs[:], in0=x[:, kk * T:(kk + 1) * T], in1=p_mb[:]),
                    e.tensor_mul(out=lnT[:, kk * T:(kk + 1) * T], in0=sqs[:],
                                 in1=p_rb[:]).then_inc(p.s_dve, 1)))
        ln_dve = p.ct["dve"]
        if dbg:
            p.inc("sto", 16)
            p.add("gpsimd", lambda e, v=ln_dve: (
                W(p, e, ("dve", v)),
                e.dma_start(out=d_ln, in_=lnT[:]).then_inc(p.s_sto, 16)))

        with contextlib.ExitStack() as q3:
            p_l = ps("p_l", [8, T], s=q3)
            p_tp = ps("p_tp", [128, 128], s=q3)
            wrldv = p.inc("wld", 16)
            p.add("sync", lambda e, poolv=p.ct["pool"]: (
                W(p, e, ("pool", poolv)),
                e.dma_start(out=wbuff[:, 0:8 * 16].rearrange("p (k c) -> p k c", c=8),
                            in_=wr.rearrange("k p c -> p k c"))
                 .then_inc(p.s_wld, 16)))
            plv = p.inc("pool")
            p.add("gpsimd", lambda e, wrldv=wrldv, pev=p.ct["pe"]: (
                W(p, e, ("wld", wrldv), ("pe", pev)),
                e.tensor_copy(out=wbufb[:, 0:8 * 16], in_=wbuff[:, 0:8 * 16])
                 .then_inc(p.s_pool, 1)))
            pv = p.inc("pe")
            p.add("tensor", lambda e, plv=plv, lv=ln_dve: (
                W(p, e, ("pool", plv), ("dve", lv)),
                [e.matmul(p_l[:, tt * 512:(tt + 1) * 512],
                          wbufb[:, kk * 8:(kk + 1) * 8],
                          lnT[:, kk * T + tt * 512: kk * T + tt * 512 + 512],
                          start=(kk == 0), stop=(kk == 15))
                 for tt in range(2) for kk in range(16)][-1].then_inc(p.s_pe, 1)))
            dv = p.inc("dve")
            p.add("vector", lambda e, pv=pv: (
                W(p, e, ("pe", pv)),
                e.tensor_copy(out=rl[:], in_=p_l[:]).then_inc(p.s_dve, 1)))
            for j in range(8):
                dvw = p.ct["dve"]
                tpe = p.inc("pe")
                p.add("tensor", lambda e, j=j, dvw=dvw: (
                    W(p, e, ("dve", dvw)),
                    e.transpose(p_tp[:, 0:8], rl[:, j * 128:(j + 1) * 128],
                                sidf[0:8, 0:8]).then_inc(p.s_pe, 1)))
                dv = p.inc("dve")
                p.add("vector", lambda e, j=j, tpe=tpe: (
                    W(p, e, ("pe", tpe)),
                    e.tensor_copy(out=probs[:, j * 8:(j + 1) * 8], in_=p_tp[:, 0:8])
                     .then_inc(p.s_dve, 1)))
            def stage(fn, n=8):
                dv = p.inc("dve")
                p.add("vector", lambda e, fn=fn: (
                    [fn(e, j) for j in range(8)][-1], e.engine_nop()
                    .then_inc(p.s_dve, 1)))
                p.fence()

            ez = lambda j: probs[:, j * 8:(j + 1) * 8]
            ind = lambda j: cbs[:, j * 32: j * 32 + 8]
            msk = lambda j: cbs[:, j * 32 + 8: j * 32 + 16]

            stage(lambda e, j: e.reduce_max(out=colsc[:, j:j + 1], in_=ez(j),
                                            axis=AX.X))
            dv = p.inc("dve")
            p.add("vector", lambda e: (
                e.tensor_scalar_mul(out=combT[0:8, 0:8] if False else colsc[:, 0:8],
                                    in0=colsc[:, 0:8], scalar1=1.0)
                 .then_inc(p.s_dve, 1)))
            p.fence()
            stage(lambda e, j: e.tensor_scalar(out=ez(j), in0=ez(j),
                                               scalar1=colsc[:, j:j + 1],
                                               scalar2=None, op0=ALU.subtract))
            for j in range(8):
                dvv = p.ct["dve"]
                av = p.inc("act")
                p.add("scalar", lambda e, j=j, dvv=dvv: (
                    W(p, e, ("dve", dvv)),
                    e.activation(out=ez(j), in_=ez(j), func=AF.Exp)
                     .then_inc(p.s_act, 1)))
            avv = p.ct["act"]
            p.add("vector", lambda e, avv=avv: W(p, e, ("act", avv)))
            stage(lambda e, j: e.tensor_scalar(out=ind(j), in0=ez(j), scalar1=1.0,
                                               scalar2=None, op0=ALU.is_ge))
            stage(lambda e, j: e.tensor_mul(out=msk(j), in0=ind(j), in1=ez(j)))
            stage(lambda e, j: e.tensor_sub(out=msk(j), in0=ez(j), in1=msk(j)))
            stage(lambda e, j: e.reduce_max(out=colsc[:, j:j + 1], in_=msk(j),
                                            axis=AX.X))
            stage(lambda e, j: e.tensor_scalar(out=ind(j), in0=ez(j),
                                               scalar1=colsc[:, j:j + 1],
                                               scalar2=None, op0=ALU.is_ge))
            stage(lambda e, j: e.tensor_mul(out=ind(j), in0=ind(j), in1=ez(j)))
            dv = p.inc("dve")
            p.add("vector", lambda e: (
                e.tensor_scalar_add(out=cbs[:, 512:520], in0=colsc[:, 0:8],
                                    scalar1=1.0).then_inc(p.s_dve, 1)))
            p.fence()
            dv = p.inc("dve")
            p.add("vector", lambda e: (
                e.reciprocal(out=colsc[:, 0:8], in_=cbs[:, 512:520])
                 .then_inc(p.s_dve, 1)))
            p.fence()
            stage(lambda e, j: e.tensor_scalar(out=ez(j), in0=ind(j),
                                               scalar1=colsc[:, j:j + 1],
                                               scalar2=None, op0=ALU.mult))

            for j in range(8):
                dvw = p.ct["dve"]
                tpe = p.inc("pe")
                p.add("tensor", lambda e, j=j, dvw=dvw: (
                    W(p, e, ("dve", dvw)),
                    e.transpose(p_tp[0:8, 0:128], probs[:, j * 8:(j + 1) * 8],
                                sidf[:]).then_inc(p.s_pe, 1)))
                dv = p.inc("dve")
                p.add("vector", lambda e, j=j, tpe=tpe: (
                    W(p, e, ("pe", tpe)),
                    e.tensor_copy(out=combT[:, j * 128:(j + 1) * 128],
                                  in_=p_tp[0:8, 0:128]).then_inc(p.s_dve, 1)))


        if dbg:
            dvv = p.ct["dve"]
            p.inc("sto", 32)
            p.add("gpsimd", lambda e, dvv=dvv: (
                W(p, e, ("dve", dvv)),
                e.dma_start(out=d_rl, in_=rl[:]).then_inc(p.s_sto, 16),
                e.dma_start(out=d_comb, in_=combT[:]).then_inc(p.s_sto, 16)))

        with contextlib.ExitStack() as q4:
            p_mm = ps("p_mm2", [128, T], s=q4)
            p_cb = ps("p_cb", [128, T], s=q4)

            def up_down(eidx):
                wup = ws1 if eidx < 0 else w1[eidx]
                wdn = ws2 if eidx < 0 else w2[eidx]
                if eidx >= 0:
                    cdv = p.ct["dve"]
                    pev = p.ct["pe"]
                    clv = p.inc("ld", 16)
                    p.add("scalar", lambda e, eidx=eidx, cdv=cdv, pev=pev: (
                        W(p, e, ("dve", cdv), ("pe", pev)),
                        e.dma_start(out=combE[0:1, :], in_=combT[eidx:eidx + 1, :])
                         .then_inc(p.s_ld, 16)))
                    cpe = p.inc("pe")
                    p.add("tensor", lambda e, clv=clv: (
                        W(p, e, ("ld", clv)),
                        [e.matmul(p_cb[:, tt * 512:(tt + 1) * 512], onesrow[:],
                                  combE[0:1, tt * 512:(tt + 1) * 512],
                                  start=True, stop=True) for tt in range(2)][-1]
                        .then_inc(p.s_pe, 1)))
                    cbdv = p.inc("dve")
                    p.add("vector", lambda e, cpe=cpe: (
                        W(p, e, ("pe", cpe)),
                        e.tensor_copy(out=cbs[:], in_=p_cb[:]).then_inc(p.s_dve, 1)))
                for mi in range(11):
                    poolv = p.ct["pool"]
                    pev = p.ct["pe"]
                    wldv = p.inc("wld", 16)
                    p.add("sync", lambda e, wup=wup, mi=mi, poolv=poolv: (
                        W(p, e, ("pool", poolv)),
                        e.dma_start(out=wbuff[:], in_=wup[mi]).then_inc(p.s_wld, 16)))
                    plv = p.inc("pool")
                    p.add("gpsimd", lambda e, wldv=wldv, pev=pev: (
                        W(p, e, ("wld", wldv), ("pe", pev)),
                        e.tensor_copy(out=wbufb[:], in_=wbuff[:])
                         .then_inc(p.s_pool, 1)))
                    dvv = p.ct["dve"]
                    avv = p.ct["act"]
                    pv = p.inc("pe")
                    p.add("tensor", lambda e, plv=plv, avv=avv: (
                        W(p, e, ("pool", plv), ("act", avv)),
                        [e.matmul(p_mm[:, tt * 512:(tt + 1) * 512],
                                  wbufb[:, kk * 128:(kk + 1) * 128],
                                  lnT[:, kk * T + tt * 512: kk * T + tt * 512 + 512],
                                  start=(kk == 0), stop=(kk == 15))
                         for tt in range(2) for kk in range(16)][-1]
                        .then_inc(p.s_pe, 1)))
                    for tt in range(2):
                        dvw = p.ct["dve"]
                        av = p.inc("act")
                        p.add("scalar", lambda e, pv=pv, tt=tt, mi=mi, dvw=dvw: (
                            W(p, e, ("pe", pv), ("dve", dvw)),
                            e.activation(out=st1[:], in_=p_mm[:, tt * 512:(tt + 1) * 512],
                                         func=AF.Silu).then_inc(p.s_act, 1)))
                        dv = p.inc("dve")
                        if eidx >= 0:
                            p.add("vector", lambda e, av=av, mi=mi, tt=tt: (
                                W(p, e, ("act", av)),
                                e.tensor_mul(out=mid[:, mi * T + tt * 512:
                                                 mi * T + tt * 512 + 512],
                                             in0=st1[:],
                                             in1=cbs[:, tt * 512:(tt + 1) * 512])
                                 .then_inc(p.s_dve, 1)))
                        else:
                            p.add("vector", lambda e, av=av, mi=mi, tt=tt: (
                                W(p, e, ("act", av)),
                                e.tensor_copy(out=mid[:, mi * T + tt * 512:
                                                  mi * T + tt * 512 + 512],
                                              in_=st1[:]).then_inc(p.s_dve, 1)))
                mid_dve = p.ct["dve"]
                for mo in range(16):
                    poolv = p.ct["pool"]
                    pev = p.ct["pe"]
                    wldv = p.inc("wld", 16)
                    p.add("sync", lambda e, wdn=wdn, mo=mo, poolv=poolv: (
                        W(p, e, ("pool", poolv)),
                        e.dma_start(out=wbuff[:, 0:11 * 128], in_=wdn[mo])
                         .then_inc(p.s_wld, 16)))
                    plv = p.inc("pool")
                    p.add("gpsimd", lambda e, wldv=wldv, pev=pev: (
                        W(p, e, ("wld", wldv), ("pe", pev)),
                        e.tensor_copy(out=wbufb[:, 0:11 * 128], in_=wbuff[:, 0:11 * 128])
                         .then_inc(p.s_pool, 1)))
                    dvv = p.ct["dve"]
                    pv = p.inc("pe")
                    p.add("tensor", lambda e, plv=plv, mdv=mid_dve, dvv=dvv: (
                        W(p, e, ("pool", plv), ("dve", mdv), ("dve", dvv)),
                        [e.matmul(p_mm[:, tt * 512:(tt + 1) * 512],
                                  wbufb[:, kk * 128:(kk + 1) * 128],
                                  mid[:, kk * T + tt * 512: kk * T + tt * 512 + 512],
                                  start=(kk == 0), stop=(kk == 10))
                         for tt in range(2) for kk in range(11)][-1]
                        .then_inc(p.s_pe, 1)))
                    dv = p.inc("dve")
                    p.add("vector", lambda e, pv=pv, mo=mo: (
                        W(p, e, ("pe", pv)),
                        e.tensor_add(out=x[:, mo * T:(mo + 1) * T],
                                     in0=x[:, mo * T:(mo + 1) * T], in1=p_mm[:])
                         .then_inc(p.s_dve, 1)))

            up_down(-1)
            if dbg:
                dvv = p.ct["dve"]
                p.inc("sto", 16)
                p.add("gpsimd", lambda e, dvv=dvv: (
                    W(p, e, ("dve", dvv)),
                    e.dma_start(out=d_mid, in_=mid[:]).then_inc(p.s_sto, 16)))
            for eidx in range(8):
                up_down(eidx)

        fdv = p.ct["dve"]
        stv = p.inc("sto", 16)
        p.add("gpsimd", lambda e, fdv=fdv: (
            W(p, e, ("dve", fdv)),
            e.dma_start(out=xo.rearrange("(k p) t -> p k t", p=128),
                        in_=x[:].rearrange("p (k t) -> p k t", t=T))
             .then_inc(p.s_sto, 16)))
        p.add("sync", lambda e, v=p.ct["sto"]: W(p, e, ("sto", v)))

        p.build(block)
    return nc


def stage_b(inputs, core, afm_outs):
    s, half = core // 2, core % 2
    x = np.asarray(inputs["L0_attn_x"])[s]
    xT = np.ascontiguousarray(x.T[:, half * 1024:(half + 1) * 1024])
    a0 = np.asarray(afm_outs[2 * s])[:, half * 1024:(half + 1) * 1024]
    a1 = np.asarray(afm_outs[2 * s + 1])[:, half * 1024:(half + 1) * 1024]
    ofm = np.ascontiguousarray(np.concatenate([a0, a1], 0))
    return dict(
        xT=xT, ofm=ofm,
        wo=_tile_lhsT(np.asarray(inputs["L0_attn_Wo"])),
        ws1=_tile_lhsT(np.asarray(inputs["L0_ffn_w_shared1"])),
        ws2=_tile_lhsT(np.asarray(inputs["L0_ffn_w_shared2"])),
        w1=np.ascontiguousarray(np.stack(
            [_tile_lhsT(np.asarray(inputs["L0_ffn_w1"])[e]) for e in range(8)])),
        w2=np.ascontiguousarray(np.stack(
            [_tile_lhsT(np.asarray(inputs["L0_ffn_w2"])[e]) for e in range(8)])),
        wr=np.ascontiguousarray(np.asarray(inputs["L0_ffn_w_router"])
                                .reshape(16, 128, 8)),
        identin=np.eye(128, dtype=np.float32),
    )


def run_launch_b(inputs, afm_outs):
    nc = _get("b", build_launch_b)
    maps = [stage_b(inputs, c, afm_outs) for c in range(NCORES)]
    res = run_bass_kernel_spmd(nc, maps, core_ids=list(range(NCORES)))
    return [np.asarray(r["xo"]) for r in res.results]




def build_launch_c(dbg=False):
    nc = bass.Bass("TRN2", target_bir_lowering=False, debug=False)
    xT = nc.dram_tensor("xT", [D, S], F32, kind="ExternalInput").ap()
    wq = nc.dram_tensor("wq", [8, 128, 16, 128], F32, kind="ExternalInput").ap()
    wk = nc.dram_tensor("wk", [8, 128, 16, 128], F32, kind="ExternalInput").ap()
    wv = nc.dram_tensor("wv", [8, 128, 16, 128], F32, kind="ExternalInput").ap()
    cost = nc.dram_tensor("cost", [128, S], F32, kind="ExternalInput").ap()
    sint = nc.dram_tensor("sint", [128, S], F32, kind="ExternalInput").ap()
    permT = nc.dram_tensor("permT", [128, 128], F32, kind="ExternalInput").ap()
    maskc = nc.dram_tensor("maskc", [128, 128], F32, kind="ExternalInput").ap()
    identin = nc.dram_tensor("identin", [128, 128], F32, kind="ExternalInput").ap()
    afm = nc.dram_tensor("afm", [HPC * 128, S], BF16, kind="ExternalOutput").ap()
    if dbg:
        d_q = nc.dram_tensor("d_q", [128, 8 * S], BF16, kind="ExternalOutput").ap()
        d_k = nc.dram_tensor("d_k", [128, 8 * S], BF16, kind="ExternalOutput").ap()
        d_v = nc.dram_tensor("d_v", [128, 16 * 1024], BF16, kind="ExternalOutput").ap()
        d_lnc = nc.dram_tensor("d_lnc", [128, 16 * S], BF16, kind="ExternalOutput").ap()

    SC = 1.0 / np.sqrt(128.0)

    with contextlib.ExitStack() as stk, nc.Block() as block:
        p = Prog(nc, stk)
        sb = lambda name, shape, dt=F32, s=stk: s.enter_context(nc.sbuf_tensor(name, shape, dt))
        ps = lambda name, shape, dt=F32, s=stk: s.enter_context(nc.psum_tensor(name, shape, dt))

        qT = sb("qT", [128, 8 * S], BF16)
        kT = sb("kT", [128, 8 * S], BF16)
        V = sb("V", [128, 16 * 1024], BF16)
        kaug = sb("kaug", [65, S], BF16)
        qaug = sb("qaug", [65, S], BF16)
        smask = sb("smask", [128, 128])
        sident = sb("sident", [128, 128], BF16)
        sidf = sb("sidf", [128, 128], F32)
        onescol = sb("onescol", [128, 2])
        onescolb = sb("onescolb", [128, 2], BF16)
        onesrow = sb("onesrow", [1, 128])
        vtmp = sb("vtmp", [128, 512], BF16)
        rtmp = sb("rtmp", [128, 512], BF16)
        spermb = sb("spermb", [128, 128], BF16)

        ldv = p.inc("ld", 48)
        p.add("sync", lambda e: (
            e.dma_start(out=sidf[:], in_=identin).then_inc(p.s_ld, 16),
            e.dma_start(out=smask[:], in_=maskc).then_inc(p.s_ld, 16),
            e.dma_start(out=vtmp[:].bitcast(F32)[:, 0:128], in_=permT)
             .then_inc(p.s_ld, 16)))
        dv = p.inc("dve")
        p.add("vector", lambda e, v=ldv: (
            W(p, e, ("ld", v)),
            e.memset(onescol[:], 1.0),
            e.memset(onescolb[:], 1.0),
            e.memset(onesrow[:], 1.0),
            e.memset(kaug[64:65, :], 1.0),
            e.tensor_copy(out=sident[:], in_=sidf[:]),
            e.tensor_copy(out=spermb[:], in_=vtmp[:].bitcast(F32)[:, 0:128])
             .then_inc(p.s_dve, 1)))
        const_dve = p.ct["dve"]

        with contextlib.ExitStack() as s1:
            lnT = sb("lnT", [128, 16 * S], BF16, s=s1)
            xchunk = sb("xchunk", [128, S], F32, s=s1)
            wbuff = sb("wbuff", [128, 16 * 128], F32, s=s1)
            wbufb = sb("wbufb", [128, 16 * 128], BF16, s=s1)
            ctab = sb("ctab", [128, S], BF16, s=s1)
            stab = sb("stab", [128, S], BF16, s=s1)

            def lrow(name, n=S, off=0):
                if name == "t1":
                    return lnT[:].bitcast(F32)[0:1, off:off + n]
                if name == "rstd":
                    return lnT[:].bitcast(F32)[0:1, S + off: S + off + n]
                buf = {"mean": xchunk, "t0": wbuff}[name]
                return buf[0:1, off:off + n]

            tldv = p.inc("ld", 32)
            p.add("sync", lambda e, cv=const_dve: (
                e.dma_start(out=wbuff[:, 0:S], in_=cost).then_inc(p.s_ld, 16),
                e.dma_start(out=lnT[:].bitcast(F32)[:, 0:S], in_=sint)
                 .then_inc(p.s_ld, 16)))
            dv = p.inc("dve")
            p.add("vector", lambda e, tv=tldv: (
                W(p, e, ("ld", tv)),
                e.tensor_copy(out=ctab[:], in_=wbuff[:, 0:S]),
                e.tensor_copy(out=stab[:], in_=lnT[:].bitcast(F32)[:, 0:S])
                 .then_inc(p.s_dve, 1)))
            tab_dve = p.ct["dve"]

            with contextlib.ExitStack() as q1:
                p_su = ps("p_su", [1, S], s=q1)
                p_sq = ps("p_sq", [1, S], s=q1)
                for kk in range(16):
                    pev = p.ct["pe"]
                    ldv = p.inc("ld", 16)
                    p.add("sync", lambda e, kk=kk, pev=pev, tv=tab_dve: (
                        W(p, e, ("pe", pev), ("dve", tv)),
                        e.dma_start(out=xchunk[:], in_=xT[kk * 128:(kk + 1) * 128, :])
                         .then_inc(p.s_ld, 16)))
                    av = p.inc("act")
                    p.add("scalar", lambda e, ldv=ldv, pev=pev, tv=tab_dve: (
                        W(p, e, ("ld", ldv), ("pe", pev), ("dve", tv)),
                        e.activation(out=wbuff[:, 0:S], in_=xchunk[:], func=AF.Square)
                         .then_inc(p.s_act, 1)))
                    pv = p.inc("pe")
                    p.add("tensor", lambda e, kk=kk, av=av: (
                        W(p, e, ("act", av)),
                        [e.matmul(p_su[0:1, tt * 512:(tt + 1) * 512], onescol[:, 0:1],
                                  xchunk[:, tt * 512:(tt + 1) * 512],
                                  start=(kk == 0), stop=(kk == 15)) for tt in range(4)],
                        [e.matmul(p_sq[0:1, tt * 512:(tt + 1) * 512], onescol[:, 0:1],
                                  wbuff[:, tt * 512:(tt + 1) * 512],
                                  start=(kk == 0), stop=(kk == 15)) for tt in range(4)][-1]
                        .then_inc(p.s_pe, 1)))
                pev = p.ct["pe"]
                dv = p.inc("dve")
                p.add("vector", lambda e, pev=pev: (
                    W(p, e, ("pe", pev)),
                    e.tensor_scalar_mul(out=lrow("mean"), in0=p_su[0:1, :], scalar1=1.0 / D),
                    e.tensor_scalar_mul(out=lrow("t0"), in0=p_sq[0:1, :], scalar1=1.0 / D),
                    e.tensor_mul(out=lrow("t1"), in0=lrow("mean"), in1=lrow("mean")),
                    e.tensor_sub(out=lrow("t0"), in0=lrow("t0"), in1=lrow("t1")),
                    e.tensor_scalar_add(out=lrow("t0"), in0=lrow("t0"), scalar1=EPS)
                     .then_inc(p.s_dve, 1)))
                av = p.inc("act")
                p.add("scalar", lambda e, dv=dv: (
                    W(p, e, ("dve", dv)),
                    e.activation(out=lrow("t1"), in_=lrow("t0"), func=AF.Sqrt)
                     .then_inc(p.s_act, 1)))
                dv = p.inc("dve")
                p.add("vector", lambda e, av=av: (
                    W(p, e, ("act", av)),
                    e.reciprocal(out=lrow("rstd"), in_=lrow("t1")).then_inc(p.s_dve, 1)))
                p.fence()
                stats_dve = p.ct["dve"]

            with contextlib.ExitStack() as q2:
                p_mb = ps("p_mb", [128, S], s=q2)
                p_rb = ps("p_rb", [128, S], s=q2)
                bcast_pe = p.inc("pe")
                p.add("tensor", lambda e, dv=stats_dve: (
                    W(p, e, ("dve", dv)),
                    [e.matmul(p_mb[:, tt * 512:(tt + 1) * 512], onesrow[:],
                              lrow("mean", 512, tt * 512), start=True, stop=True)
                     for tt in range(4)],
                    [e.matmul(p_rb[:, tt * 512:(tt + 1) * 512], onesrow[:],
                              lrow("rstd", 512, tt * 512), start=True, stop=True)
                     for tt in range(4)][-1].then_inc(p.s_pe, 1)))
                for kk in range(16):
                    dvp = p.ct["dve"]
                    ldv = p.inc("ld", 16)
                    p.add("sync", lambda e, kk=kk, dvp=dvp, bp=bcast_pe: (
                        W(p, e, ("dve", dvp), ("pe", bp)),
                        e.dma_start(out=xchunk[:], in_=xT[kk * 128:(kk + 1) * 128, :])
                         .then_inc(p.s_ld, 16)))
                    dv = p.inc("dve")
                    p.add("vector", lambda e, kk=kk, ldv=ldv, bp=bcast_pe: (
                        W(p, e, ("ld", ldv), ("pe", bp)),
                        e.tensor_sub(out=wbuff[:, 0:S], in0=xchunk[:], in1=p_mb[:]),
                        e.tensor_mul(out=lnT[:, kk * S:(kk + 1) * S], in0=wbuff[:, 0:S],
                                     in1=p_rb[:]).then_inc(p.s_dve, 1)))
                ln_dve = p.ct["dve"]

            if dbg:
                dvv = p.ct["dve"]
                p.inc("sto", 16)
                p.add("gpsimd", lambda e, dvv=dvv: (
                    W(p, e, ("dve", dvv)),
                    e.dma_start(out=d_lnc, in_=lnT[:]).then_inc(p.s_sto, 16)))

            with contextlib.ExitStack() as q4:
                p_mm = ps("p_mm", [128, S], s=q4)
                p_tp = ps("p_tp", [128, 128], BF16, s=q4)

                def proj(wsrc, m):
                    poolv = p.ct["pool"]
                    pev = p.ct["pe"]
                    dvw = p.ct["dve"]
                    wldv = p.inc("wld", 16)
                    p.add("sync", lambda e, wsrc=wsrc, m=m, poolv=poolv, dvw=dvw: (
                        W(p, e, ("pool", poolv), ("dve", dvw)),
                        e.dma_start(out=wbuff[:], in_=wsrc[m]).then_inc(p.s_wld, 16)))
                    plv = p.inc("pool")
                    p.add("gpsimd", lambda e, wldv=wldv, pev=pev: (
                        W(p, e, ("wld", wldv), ("pe", pev)),
                        e.tensor_copy(out=wbufb[:], in_=wbuff[:]).then_inc(p.s_pool, 1)))
                    dvv = p.ct["dve"]
                    pv = p.inc("pe")
                    p.add("tensor", lambda e, plv=plv, dvv=dvv, lv=ln_dve: (
                        W(p, e, ("pool", plv), ("dve", dvv), ("dve", lv)),
                        [e.matmul(p_mm[:, tt * 512:(tt + 1) * 512],
                                  wbufb[:, kk * 128:(kk + 1) * 128],
                                  lnT[:, kk * S + tt * 512: kk * S + tt * 512 + 512],
                                  start=(kk == 0), stop=(kk == 15))
                         for tt in range(4) for kk in range(16)][-1]
                        .then_inc(p.s_pe, 1)))
                    return pv

                def rope_evac(dest, m, pv):
                    dv1 = p.inc("dve")
                    p.add("vector", lambda e, pv=pv: (
                        W(p, e, ("pe", pv)),
                        e.tensor_copy(out=kaug[0:64, :].bitcast(F32)
                                      if False else xchunk[:].bitcast(BF16)[:, 0:S],
                                      in_=p_mm[:]).then_inc(p.s_dve, 1)))
                    pv2 = p.inc("pe")
                    p.add("tensor", lambda e, dv1=dv1: (
                        W(p, e, ("dve", dv1)),
                        [e.matmul(p_mm[:, tt * 512:(tt + 1) * 512], spermb[:],
                                  xchunk[:].bitcast(BF16)[:, tt * 512:(tt + 1) * 512],
                                  start=True, stop=True) for tt in range(4)][-1]
                        .then_inc(p.s_pe, 1)))
                    for tt in range(4):
                        dv = p.inc("dve")
                        p.add("vector", lambda e, pv2=pv2, tt=tt, m=m, dest=dest: (
                            W(p, e, ("pe", pv2)),
                            e.tensor_mul(out=rtmp[:],
                                         in0=xchunk[:].bitcast(BF16)[:, tt * 512:(tt + 1) * 512],
                                         in1=ctab[:, tt * 512:(tt + 1) * 512]),
                            e.tensor_mul(out=vtmp[:], in0=p_mm[:, tt * 512:(tt + 1) * 512],
                                         in1=stab[:, tt * 512:(tt + 1) * 512]),
                            e.tensor_add(out=dest[:, m * S + tt * 512: m * S + tt * 512 + 512],
                                         in0=rtmp[:], in1=vtmp[:])
                             .then_inc(p.s_dve, 1)))

                for wsrc, dest in ((wq, qT), (wk, kT)):
                    for m in range(8):
                        pv = proj(wsrc, m)
                        rope_evac(dest, m, pv)

                for m in range(8):
                    pv = proj(wv, m)
                    for tt in range(4):
                        pet = p.ct["pe"]
                        p.inc("dve")
                        p.add("vector", lambda e, pv=pv, pet=pet, tt=tt: (
                            W(p, e, ("pe", max(pv, pet))),
                            e.tensor_copy(out=vtmp[:],
                                          in_=p_mm[:, tt * 512:(tt + 1) * 512])
                             .then_inc(p.s_dve, 1)))
                        for j in range(4):
                            dvw = p.ct["dve"]
                            tppe = p.inc("pe")
                            p.add("tensor", lambda e, dvw=dvw, j=j: (
                                W(p, e, ("dve", dvw)),
                                e.transpose(p_tp[:], vtmp[:, j * 128:(j + 1) * 128],
                                            sident[:]).then_inc(p.s_pe, 1)))
                            p.inc("dve")
                            p.add("vector", lambda e, tppe=tppe, tt=tt, j=j, m=m: (
                                W(p, e, ("pe", tppe)),
                                e.tensor_copy(
                                    out=V[:, (tt * 4 + j) * 1024 + m * 128:
                                           (tt * 4 + j) * 1024 + (m + 1) * 128],
                                    in_=p_tp[:]).then_inc(p.s_dve, 1)))

        if dbg:
            dvv = p.ct["dve"]
            p.inc("sto", 48)
            p.add("gpsimd", lambda e, dvv=dvv: (
                W(p, e, ("dve", dvv)),
                e.dma_start(out=d_q, in_=qT[:]).then_inc(p.s_sto, 16),
                e.dma_start(out=d_k, in_=kT[:]).then_inc(p.s_sto, 16),
                e.dma_start(out=d_v, in_=V[:]).then_inc(p.s_sto, 16)))

        with contextlib.ExitStack() as s2:
            pT = sb("pT", [128, 16 * 1024], BF16, s=s2)
            gsc = sb("gsc", [128, S], s=s2)
            rbs = sb("rbs", [128, 512], s=s2)
            afm_t = sb("afm_t", [128, S], BF16, s=s2)
            arows = sb("arows", [1, 3 * S], s=s2)
            ARN = {"t0": 0, "t1": 1, "r": 2}

            def arow(name, n=S, off=0):
                return arows[0:1, ARN[name] * S + off: ARN[name] * S + off + n]

            with contextlib.ExitStack() as q5:
                p_sc = ps("p_sc", [128, 2048], s=q5)
                p_o = ps("p_o", [128, 512], s=q5)
                p_r = ps("p_r", [1, 512], s=q5)
                p_b = ps("p_b", [128, 512], s=q5)

                def rowslot(i, n=512):
                    return p_sc[0:1, i * 512: i * 512 + n]

                for h in range(8):
                    m = h
                    pev = p.ct["pe"]
                    dvv = p.ct["dve"]
                    av = p.inc("act")
                    p.add("scalar", lambda e, pev=pev, dvv=dvv, m=m: (
                        W(p, e, ("pe", pev), ("dve", dvv)),
                        e.activation(out=gsc[:], in_=qT[:, m * S:(m + 1) * S],
                                     func=AF.Square).then_inc(p.s_act, 1)))
                    pv = p.inc("pe")
                    p.add("tensor", lambda e, av=av: (
                        W(p, e, ("act", av)),
                        [e.matmul(rowslot(tt), onescol[:, 0:1],
                                  gsc[:, tt * 512:(tt + 1) * 512],
                                  start=True, stop=True) for tt in range(4)][-1]
                        .then_inc(p.s_pe, 1)))
                    dv = p.inc("dve")
                    p.add("vector", lambda e, pv=pv: (
                        W(p, e, ("pe", pv)),
                        e.tensor_copy(out=arow("t0"), in_=p_sc[0:1, 0:S])
                         .then_inc(p.s_dve, 1)))
                    av2 = p.inc("act")
                    p.add("scalar", lambda e, dv=dv, m=m: (
                        W(p, e, ("dve", dv)),
                        e.activation(out=gsc[:], in_=kT[:, m * S:(m + 1) * S],
                                     func=AF.Square).then_inc(p.s_act, 1)))
                    pv2 = p.inc("pe")
                    p.add("tensor", lambda e, av2=av2: (
                        W(p, e, ("act", av2)),
                        [e.matmul(rowslot(tt), onescol[:, 0:1],
                                  gsc[:, tt * 512:(tt + 1) * 512],
                                  start=True, stop=True) for tt in range(4)][-1]
                        .then_inc(p.s_pe, 1)))
                    dv2 = p.inc("dve")
                    p.add("vector", lambda e, pv2=pv2: (
                        W(p, e, ("pe", pv2)),
                        e.reduce_max(out=arow("t1", 1), in_=p_sc[0:1, 0:S],
                                     axis=AX.X).then_inc(p.s_dve, 1)))
                    av3 = p.inc("act")
                    p.add("scalar", lambda e, dv2=dv2: (
                        W(p, e, ("dve", dv2)),
                        e.activation(out=arow("t0"), in_=arow("t0"), func=AF.Sqrt,
                                     scale=arow("t1", 1)).then_inc(p.s_act, 1)))
                    bdv = p.inc("dve")
                    p.add("vector", lambda e, av3=av3: (
                        W(p, e, ("act", av3)),
                        e.tensor_scalar_mul(out=qaug[64:65, :], in0=arow("t0"),
                                            scalar1=-1.0).then_inc(p.s_dve, 1)))

                    pev = p.ct["pe"]
                    augv = p.inc("ld", 32)
                    p.add("scalar", lambda e, pev=pev, m=m: (
                        W(p, e, ("pe", pev)),
                        e.dma_start(out=kaug[0:64, :],
                                    in_=kT[64:128, m * S:(m + 1) * S])
                         .then_inc(p.s_ld, 16),
                        e.dma_start(out=qaug[0:64, :],
                                    in_=qT[64:128, m * S:(m + 1) * S])
                         .then_inc(p.s_ld, 16)))

                    for qh in range(2):
                        exp_av = []
                        kts = [kt for kt in range(16) if kt < 8 * (qh + 1)]
                        for kt in kts:
                            half = kt % 2
                            wav = exp_av[-2] if len(exp_av) >= 2 else 0
                            qns = [qn for qn in range(2)
                                   if (qh * 2 + qn) * 512 + 512 > kt * 128]
                            pv = p.inc("pe")
                            p.add("tensor", lambda e, kt=kt, half=half, qh=qh,
                                  augv=augv, bdv=bdv, wav=wav, qns=qns, m=m: (
                                W(p, e, ("ld", augv), ("dve", bdv), ("act", wav)),
                                [(e.matmul(p_sc[:, half * 1024 + qn * 512:
                                               half * 1024 + qn * 512 + 512],
                                           kT[0:64, m * S + kt * 128:
                                              m * S + kt * 128 + 128],
                                           qT[0:64, m * S + (qh * 2 + qn) * 512:
                                              m * S + (qh * 2 + qn) * 512 + 512],
                                           start=True, stop=False),
                                  e.matmul(p_sc[:, half * 1024 + qn * 512:
                                               half * 1024 + qn * 512 + 512],
                                           kaug[:, kt * 128:(kt + 1) * 128],
                                           qaug[:, (qh * 2 + qn) * 512:
                                                (qh * 2 + qn) * 512 + 512],
                                           start=False, stop=True))
                                 for qn in qns][-1][-1].then_inc(p.s_pe, 1)))
                            mask_ops = []
                            for qn in qns:
                                w = kt * 128 - (qh * 2 + qn) * 512
                                if w >= 0:
                                    base = half * 1024 + qn * 512
                                    mask_ops.append((base, w))
                            if mask_ops:
                                mdv = p.inc("dve")

                                def do_mask(e, pv=pv, mask_ops=tuple(mask_ops)):
                                    W(p, e, ("pe", pv))
                                    last = None
                                    for base, w in mask_ops:
                                        if w > 0:
                                            last = e.tensor_scalar_add(
                                                out=p_sc[:, base:base + w],
                                                in0=p_sc[:, base:base + w],
                                                scalar1=-30000.0)
                                        last = e.tensor_add(
                                            out=p_sc[:, base + w:base + w + 128],
                                            in0=p_sc[:, base + w:base + w + 128],
                                            in1=smask[:])
                                    last.then_inc(p.s_dve, 1)
                                p.add("vector", do_mask)
                                waitkind = ("dve", mdv)
                            else:
                                waitkind = ("pe", pv)
                            av = p.inc("act")
                            exp_av.append(av)
                            p.add("scalar", lambda e, wk_=waitkind, kt=kt, half=half: (
                                W(p, e, wk_),
                                e.activation(out=pT[:, kt * 1024:(kt + 1) * 1024],
                                             in_=p_sc[:, half * 1024: half * 1024 + 1024],
                                             func=AF.Exp, scale=SC)
                                 .then_inc(p.s_act, 1)))
                        all_exp = p.ct["act"]

                        for qn in range(2):
                            qabs = qh * 2 + qn
                            vkts = [kt for kt in kts if kt * 128 < qabs * 512 + 512]
                            dvo = p.ct["dve"]
                            pv = p.inc("pe")
                            p.add("tensor", lambda e, ae=all_exp, qn=qn, m=m,
                                  dvo=dvo, vkts=vkts: (
                                W(p, e, ("act", ae), ("dve", dvo)),
                                [e.matmul(p_o[:, :],
                                          V[:, kt2 * 1024 + m * 128:
                                             kt2 * 1024 + (m + 1) * 128],
                                          pT[:, kt2 * 1024 + qn * 512:
                                             kt2 * 1024 + qn * 512 + 512],
                                          start=(kt2 == vkts[0]), stop=(kt2 == vkts[-1]))
                                 for kt2 in vkts],
                                [e.matmul(p_r[0:1, :], onescolb[:, 0:1],
                                          pT[:, kt2 * 1024 + qn * 512:
                                             kt2 * 1024 + qn * 512 + 512],
                                          start=(kt2 == vkts[0]), stop=(kt2 == vkts[-1]))
                                 for kt2 in vkts][-1].then_inc(p.s_pe, 1)))
                            dv = p.inc("dve")
                            p.add("vector", lambda e, pv=pv, qabs=qabs: (
                                W(p, e, ("pe", pv)),
                                e.reciprocal(out=arow("r", 512), in_=p_r[0:1, :])
                                 .then_inc(p.s_dve, 1)))
                            p.fence()
                            pvb = p.inc("pe")
                            p.add("tensor", lambda e, dv=p.ct["dve"]: (
                                W(p, e, ("dve", dv)),
                                e.matmul(p_b[:], onesrow[:], arow("r", 512),
                                         start=True, stop=True).then_inc(p.s_pe, 1)))
                            stv = p.ct["sto"]
                            dv2 = p.inc("dve")
                            p.add("vector", lambda e, pvb=pvb, qabs=qabs, stv=stv: (
                                W(p, e, ("pe", pvb), ("sto", stv)),
                                e.tensor_copy(out=rbs[:], in_=p_b[:]),
                                e.tensor_mul(out=afm_t[:, qabs * 512:(qabs + 1) * 512],
                                             in0=p_o[:], in1=rbs[:])
                                 .then_inc(p.s_dve, 1)))

                    hdv = p.ct["dve"]
                    stv = p.inc("sto", 16)
                    p.add("gpsimd", lambda e, hdv=hdv, h=h: (
                        W(p, e, ("dve", hdv)),
                        e.dma_start(out=afm[h * 128:(h + 1) * 128, :], in_=afm_t[:])
                         .then_inc(p.s_sto, 16)))

        p.add("sync", lambda e, v=p.ct["sto"]: W(p, e, ("sto", v)))
        p.build(block)
    return nc


def stage_c(inputs, core, xT_full):
    s, half = core // 2, core % 2
    wq = np.asarray(inputs["L1_attn_wq"])
    wk = np.asarray(inputs["L1_attn_wk"])
    wv = np.asarray(inputs["L1_attn_wv"])
    rows = slice(1024 * half, 1024 * half + 1024)

    def tile(w):
        return _tile_lhsT(np.ascontiguousarray(w[rows, :].T))

    pos = np.arange(S, dtype=np.float64)[:, None]
    inv = 10000.0 ** (-(np.arange(0, 128, 2, dtype=np.float64) / 128.0))
    ang = pos * inv
    sin = np.repeat(np.sin(ang), 2, axis=1).T.astype(np.float32)
    cos = np.repeat(np.cos(ang), 2, axis=1).T.astype(np.float32)
    P = np.zeros((128, 128), np.float32)
    for i in range(64):
        P[2 * i, 2 * i + 1] = -1.0
        P[2 * i + 1, 2 * i] = 1.0
    mask = np.where(np.arange(128)[None, :] < np.arange(128)[:, None],
                    -30000.0, 0.0).astype(np.float32)
    return dict(
        xT=np.ascontiguousarray(xT_full[s]),
        wq=tile(wq), wk=tile(wk), wv=tile(wv),
        cost=np.ascontiguousarray(cos), sint=np.ascontiguousarray(sin),
        permT=np.ascontiguousarray(P.T),
        maskc=np.ascontiguousarray(mask),
        identin=np.eye(128, dtype=np.float32),
    )


def run_launch_c(inputs, xo_outs):
    xT_full = []
    for s in range(4):
        xT_full.append(np.concatenate(
            [np.asarray(xo_outs[2 * s]), np.asarray(xo_outs[2 * s + 1])], axis=1))
    nc = _get("c", build_launch_c)
    maps = [stage_c(inputs, c, xT_full) for c in range(NCORES)]
    res = run_bass_kernel_spmd(nc, maps, core_ids=list(range(NCORES)))
    return [np.asarray(r["afm"]) for r in res.results]




def build_launch_d(upto="full"):
    nc = bass.Bass("TRN2", target_bir_lowering=False, debug=False)
    T = 1024
    xT = nc.dram_tensor("xT", [D, T], F32, kind="ExternalInput").ap()
    ofm = nc.dram_tensor("ofm", [D, T], BF16, kind="ExternalInput").ap()
    woT = nc.dram_tensor("woT", [16, 128, 16, 128], F32, kind="ExternalInput").ap()
    wupq = nc.dram_tensor("wupq", [64, 128, 16, 128], I32, kind="ExternalInput").ap()
    wups = nc.dram_tensor("wups", [64, 128, 16, 128], F32, kind="ExternalInput").ap()
    bup = nc.dram_tensor("bup", [128, 64], F32, kind="ExternalInput").ap()
    aup = nc.dram_tensor("aup", [128, 16 * 16], F32, kind="ExternalInput").ap()
    bupl = nc.dram_tensor("bupl", [64, 16, 128], F32, kind="ExternalInput").ap()
    wdnq = nc.dram_tensor("wdnq", [16, 128, 64, 128], I32, kind="ExternalInput").ap()
    wdns = nc.dram_tensor("wdns", [16, 128, 64, 128], F32, kind="ExternalInput").ap()
    bdn = nc.dram_tensor("bdn", [128, 16], F32, kind="ExternalInput").ap()
    adn = nc.dram_tensor("adn", [128, 64 * 16], F32, kind="ExternalInput").ap()
    bdnl = nc.dram_tensor("bdnl", [16, 16, 128], F32, kind="ExternalInput").ap()
    xout = nc.dram_tensor("xout", [D, T], F32, kind="ExternalOutput").ap()
    xs = nc.dram_tensor("xs", [128, 16 * T], F32).ap()

    with contextlib.ExitStack() as stk, nc.Block() as block:
        p = Prog(nc, stk)
        sb = lambda name, shape, dt=F32, s=stk: s.enter_context(nc.sbuf_tensor(name, shape, dt))
        ps = lambda name, shape, dt=F32, s=stk: s.enter_context(nc.psum_tensor(name, shape, dt))

        lnT = sb("lnT", [128, 16 * T], BF16)
        wqi = sb("wqi", [128, 16 * 128], I32)
        wsc = sb("wsc", [128, 16 * 128])
        wfc = sb("wfc", [128, 16 * 128])
        wbufb = sb("wbufb", [128, 16 * 128], BF16)
        ybuf = sb("ybuf", [128, T])
        xck = sb("xck", [128, T])
        t1b = sb("t1b", [16, T], BF16)
        t2b = sb("t2b", [16, T], BF16)
        lwb = sb("lwb", [16, 128], BF16)
        lwf = sb("lwf", [16, 128])
        sbup = sb("sbup", [128, 64])
        sbdn = sb("sbdn", [128, 16])
        onescol = sb("onescol", [128, 2])
        onesrow = sb("onesrow", [1, 128])

        def lrow(name, n=T, off=0):
            if name == "mean":
                return wqi[:].bitcast(F32)[0:1, off:off + n]
            if name == "t0":
                return wqi[:].bitcast(F32)[0:1, T + off:T + off + n]
            if name == "t1":
                return wsc[0:1, off:off + n]
            return wsc[0:1, T + off:T + off + n]

        ldv = p.inc("ld", 32)
        p.add("sync", lambda e: (
            e.dma_start(out=sbup[:], in_=bup).then_inc(p.s_ld, 16),
            e.dma_start(out=sbdn[:], in_=bdn).then_inc(p.s_ld, 16)))
        dv = p.inc("dve")
        p.add("vector", lambda e, v=ldv: (
            W(p, e, ("ld", v)),
            e.memset(onescol[:], 1.0),
            e.memset(onesrow[:], 1.0).then_inc(p.s_dve, 1)))

        with contextlib.ExitStack() as sx:
            x = sb("x", [128, 16 * T], s=sx)
            with contextlib.ExitStack() as s0:
                ofms = sb("ofms", [128, 16 * T], BF16, s=s0)
                xldv = p.inc("ld", 32)
                p.add("sync", lambda e: (
                    e.dma_start(out=x[:].rearrange("p (k t) -> p k t", t=T),
                                in_=xT.rearrange("(k p) t -> p k t", p=128))
                     .then_inc(p.s_ld, 16),
                    e.dma_start(out=ofms[:].rearrange("p (k t) -> p k t", t=T),
                                in_=ofm.rearrange("(k p) t -> p k t", p=128))
                     .then_inc(p.s_ld, 16)))
                with contextlib.ExitStack() as q0:
                    p_mm = ps("p_mmw", [128, T], s=q0)
                    for m in range(16):
                        poolv = p.ct["pool"]
                        pev = p.ct["pe"]
                        wldv = p.inc("wld", 16)
                        p.add("sync", lambda e, m=m, poolv=poolv: (
                            W(p, e, ("pool", poolv)),
                            e.dma_start(out=wfc[:], in_=woT[m]).then_inc(p.s_wld, 16)))
                        plv = p.inc("pool")
                        p.add("gpsimd", lambda e, wldv=wldv, pev=pev: (
                            W(p, e, ("wld", wldv), ("pe", pev)),
                            e.tensor_copy(out=wbufb[:], in_=wfc[:]).then_inc(p.s_pool, 1)))
                        dvv = p.ct["dve"]
                        pv = p.inc("pe")
                        p.add("tensor", lambda e, plv=plv, dvv=dvv, xv=xldv: (
                            W(p, e, ("pool", plv), ("dve", dvv), ("ld", xv)),
                            [e.matmul(p_mm[:, tt * 512:(tt + 1) * 512],
                                      wbufb[:, kk * 128:(kk + 1) * 128],
                                      ofms[:, kk * T + tt * 512: kk * T + tt * 512 + 512],
                                      start=(kk == 0), stop=(kk == 15))
                             for tt in range(2) for kk in range(16)][-1]
                            .then_inc(p.s_pe, 1)))
                        dv = p.inc("dve")
                        p.add("vector", lambda e, pv=pv, m=m: (
                            W(p, e, ("pe", pv)),
                            e.tensor_add(out=x[:, m * T:(m + 1) * T],
                                         in0=x[:, m * T:(m + 1) * T], in1=p_mm[:])
                             .then_inc(p.s_dve, 1)))
            wo_done = p.ct["dve"]

            with contextlib.ExitStack() as q1:
                p_su = ps("p_su", [1, T], s=q1)
                p_sq = ps("p_sq", [1, T], s=q1)
                for kk in range(16):
                    pev = p.ct["pe"]
                    av = p.inc("act")
                    p.add("scalar", lambda e, kk=kk, wv=wo_done, pev=pev: (
                        W(p, e, ("dve", wv), ("pe", pev)),
                        e.activation(out=ybuf[:], in_=x[:, kk * T:(kk + 1) * T],
                                     func=AF.Square).then_inc(p.s_act, 1)))
                    pv = p.inc("pe")
                    p.add("tensor", lambda e, kk=kk, av=av: (
                        W(p, e, ("act", av)),
                        [e.matmul(p_su[0:1, tt * 512:(tt + 1) * 512], onescol[:, 0:1],
                                  x[:, kk * T + tt * 512: kk * T + tt * 512 + 512],
                                  start=(kk == 0), stop=(kk == 15)) for tt in range(2)],
                        [e.matmul(p_sq[0:1, tt * 512:(tt + 1) * 512], onescol[:, 0:1],
                                  ybuf[:, tt * 512:(tt + 1) * 512],
                                  start=(kk == 0), stop=(kk == 15)) for tt in range(2)][-1]
                        .then_inc(p.s_pe, 1)))
                pev = p.ct["pe"]
                dv = p.inc("dve")
                p.add("vector", lambda e, pev=pev: (
                    W(p, e, ("pe", pev)),
                    e.tensor_scalar_mul(out=lrow("mean"), in0=p_su[0:1, :], scalar1=1.0 / D),
                    e.tensor_scalar_mul(out=lrow("t0"), in0=p_sq[0:1, :], scalar1=1.0 / D),
                    e.tensor_mul(out=lrow("t1"), in0=lrow("mean"), in1=lrow("mean")),
                    e.tensor_sub(out=lrow("t0"), in0=lrow("t0"), in1=lrow("t1")),
                    e.tensor_scalar_add(out=lrow("t0"), in0=lrow("t0"), scalar1=EPS)
                     .then_inc(p.s_dve, 1)))
                av = p.inc("act")
                p.add("scalar", lambda e, dv=dv: (
                    W(p, e, ("dve", dv)),
                    e.activation(out=lrow("t1"), in_=lrow("t0"), func=AF.Sqrt)
                     .then_inc(p.s_act, 1)))
                dv = p.inc("dve")
                p.add("vector", lambda e, av=av: (
                    W(p, e, ("act", av)),
                    e.reciprocal(out=lrow("rstd"), in_=lrow("t1")).then_inc(p.s_dve, 1)))
                p.fence()
            with contextlib.ExitStack() as q2:
                p_mb = ps("p_mb", [128, T], s=q2)
                p_rb = ps("p_rb", [128, T], s=q2)
                sdv = p.ct["dve"]
                bpe = p.inc("pe")
                p.add("tensor", lambda e, sdv=sdv: (
                    W(p, e, ("dve", sdv)),
                    [e.matmul(p_mb[:, tt * 512:(tt + 1) * 512], onesrow[:],
                              lrow("mean", 512, tt * 512), start=True, stop=True)
                     for tt in range(2)],
                    [e.matmul(p_rb[:, tt * 512:(tt + 1) * 512], onesrow[:],
                              lrow("rstd", 512, tt * 512), start=True, stop=True)
                     for tt in range(2)][-1].then_inc(p.s_pe, 1)))
                for kk in range(16):
                    dv = p.inc("dve")
                    p.add("vector", lambda e, kk=kk, bpe=bpe: (
                        W(p, e, ("pe", bpe)),
                        e.tensor_sub(out=ybuf[:], in0=x[:, kk * T:(kk + 1) * T],
                                     in1=p_mb[:]),
                        e.tensor_mul(out=lnT[:, kk * T:(kk + 1) * T], in0=ybuf[:],
                                     in1=p_rb[:]).then_inc(p.s_dve, 1)))
            ln_dve = p.ct["dve"]

            spv = p.inc("sto", 16)
            p.add("gpsimd", lambda e, v=ln_dve: (
                W(p, e, ("dve", v)),
                e.dma_start(out=xs, in_=x[:]).then_inc(p.s_sto, 16)))
            p.add("sync", lambda e, v=p.ct["sto"]: W(p, e, ("sto", v)))
            if upto == "wo":
                p.inc("sto", 16)
                p.add("gpsimd", lambda e, v=ln_dve: (
                    W(p, e, ("dve", v)),
                    e.dma_start(out=xout.rearrange("(k p) t -> p k t", p=128),
                                in_=x[:].rearrange("p (k t) -> p k t", t=T))
                     .then_inc(p.s_sto, 16)))
                p.add("sync", lambda e, v=p.ct["sto"]: W(p, e, ("sto", v)))
        if upto == "wo":
            p.build(block)
            return nc

        with contextlib.ExitStack() as s2:
            x2 = sb("x2", [128, 64 * T], BF16, s=s2)

            with contextlib.ExitStack() as q3:
                p_t = ps("p_t", [16, T], s=q3)
                p.inc("wld", 16)
                p.add("sync", lambda e, poolv=p.ct["pool"]: (
                    W(p, e, ("pool", poolv)),
                    e.dma_start(out=wfc[:, 0:256], in_=aup).then_inc(p.s_wld, 16)))
                plv = p.inc("pool")
                p.add("gpsimd", lambda e, wv=p.ct["wld"], pev=p.ct["pe"]: (
                    W(p, e, ("wld", wv), ("pe", pev)),
                    e.tensor_copy(out=wbufb[:, 0:256], in_=wfc[:, 0:256])
                     .then_inc(p.s_pool, 1)))
                pv = p.inc("pe")
                p.add("tensor", lambda e, plv=plv, lv=ln_dve: (
                    W(p, e, ("pool", plv), ("dve", lv)),
                    [e.matmul(p_t[:, tt * 512:(tt + 1) * 512],
                              wbufb[:, kk * 16:(kk + 1) * 16],
                              lnT[:, kk * T + tt * 512: kk * T + tt * 512 + 512],
                              start=(kk == 0), stop=(kk == 15))
                     for tt in range(2) for kk in range(16)][-1].then_inc(p.s_pe, 1)))
                dv = p.inc("dve")
                p.add("vector", lambda e, pv=pv: (
                    W(p, e, ("pe", pv)),
                    e.tensor_copy(out=t1b[:], in_=p_t[:]).then_inc(p.s_dve, 1)))

                p_u = ps("p_u", [128, T], s=q3)
                for m in range(int(upto[2:]) if upto.startswith("up") and len(upto) > 2 else 64):
                    poolv = p.ct["pool"]
                    dvw = p.ct["dve"]
                    wldv = p.inc("wld", 48)
                    p.add("sync", lambda e, m=m, poolv=poolv, dvw=dvw: (
                        W(p, e, ("pool", poolv), ("dve", dvw)),
                        e.dma_start(out=wqi[:], in_=wupq[m]).then_inc(p.s_wld, 16),
                        e.dma_start(out=wsc[:], in_=wups[m]).then_inc(p.s_wld, 16),
                        e.dma_start(out=lwf[:], in_=bupl[m])
                         .then_inc(p.s_wld, 16)))
                    pev = p.ct["pe"]
                    plv = p.inc("pool")
                    p.add("gpsimd", lambda e, wldv=wldv, pev=pev: (
                        W(p, e, ("wld", wldv), ("pe", pev)),
                        e.tensor_copy(out=wfc[:], in_=wqi[:]).then_inc(p.s_pool, 1)))
                    dv = p.inc("dve")
                    p.add("vector", lambda e, plv=plv: (
                        W(p, e, ("pool", plv)),
                        e.tensor_scalar(out=wfc[:], in0=wfc[:], scalar1=7.5,
                                        scalar2=None, op0=ALU.subtract),
                        e.tensor_mul(out=wbufb[:], in0=wfc[:], in1=wsc[:]),
                        e.tensor_copy(out=lwb[:], in_=lwf[:])
                         .then_inc(p.s_dve, 1)))
                    pv = p.inc("pe")
                    p.add("tensor", lambda e, dv=dv, m=m: (
                        W(p, e, ("dve", dv)),
                        [([e.matmul(p_u[:, tt * 512:(tt + 1) * 512],
                                    wbufb[:, kk * 128:(kk + 1) * 128],
                                    lnT[:, kk * T + tt * 512: kk * T + tt * 512 + 512],
                                    start=(kk == 0), stop=False)
                           for kk in range(16)],
                          e.matmul(p_u[:, tt * 512:(tt + 1) * 512],
                                   lwb[:, :], t1b[:, tt * 512:(tt + 1) * 512],
                                   start=False, stop=True))
                         for tt in range(2)][-1][-1].then_inc(p.s_pe, 1)))
                    for tt in range(2):
                        av = p.inc("act")
                        p.add("scalar", lambda e, pv=pv, m=m, tt=tt: (
                            W(p, e, ("pe", pv)),
                            e.activation(out=x2[:, m * T + tt * 512: m * T + tt * 512 + 512],
                                         in_=p_u[:, tt * 512:(tt + 1) * 512],
                                         func=AF.Relu, bias=sbup[:, m:m + 1])
                             .then_inc(p.s_act, 1)))
                up_act = p.ct["act"]
                if upto.startswith("up"):
                    p.inc("sto", 16)
                    p.add("gpsimd", lambda e, v=up_act: (
                        W(p, e, ("act", v)),
                        e.dma_start(out=xout.rearrange("(k p) t -> p k t", p=128),
                                    in_=x2[:, 0:32 * T].bitcast(F32)
                                    .rearrange("p (k t) -> p k t", t=T))
                         .then_inc(p.s_sto, 16)))

            with contextlib.ExitStack() as q4:
                p_t = ps("p_t2", [16, T], s=q4)
                p_d = ps("p_d", [128, T], s=q4)
                aldv2 = p.inc("wld", 16)
                p.add("sync", lambda e, poolv=p.ct["pool"]: (
                    W(p, e, ("pool", poolv)),
                    e.dma_start(out=wfc[:, 0:1024], in_=adn).then_inc(p.s_wld, 16)))
                plv = p.inc("pool")
                p.add("gpsimd", lambda e, wv=p.ct["wld"], pev=p.ct["pe"]: (
                    W(p, e, ("wld", wv), ("pe", pev)),
                    e.tensor_copy(out=wbufb[:, 0:1024], in_=wfc[:, 0:1024])
                     .then_inc(p.s_pool, 1)))
                pv = p.inc("pe")
                p.add("tensor", lambda e, plv=plv, ua=up_act: (
                    W(p, e, ("pool", plv), ("act", ua)),
                    [e.matmul(p_t[:, tt * 512:(tt + 1) * 512],
                              wbufb[:, kk * 16:(kk + 1) * 16],
                              x2[:, kk * T + tt * 512: kk * T + tt * 512 + 512],
                              start=(kk == 0), stop=(kk == 63))
                     for tt in range(2) for kk in range(64)][-1].then_inc(p.s_pe, 1)))
                dv = p.inc("dve")
                p.add("vector", lambda e, pv=pv: (
                    W(p, e, ("pe", pv)),
                    e.tensor_copy(out=t2b[:], in_=p_t[:]).then_inc(p.s_dve, 1)))
                t2_dve = p.ct["dve"]

                for mo in range(16):
                    pvs = []
                    for kg in range(4):
                        poolv = p.ct["pool"]
                        dvw = p.ct["dve"]
                        wldv = p.inc("wld", 48)
                        p.add("sync", lambda e, mo=mo, kg=kg, poolv=poolv, dvw=dvw: (
                            W(p, e, ("pool", poolv), ("dve", dvw)),
                            e.dma_start(out=wqi[:].rearrange("p (k c) -> p k c", c=128),
                                        in_=wdnq[mo][:, kg * 16:(kg + 1) * 16, :])
                             .then_inc(p.s_wld, 16),
                            e.dma_start(out=wsc[:].rearrange("p (k c) -> p k c", c=128),
                                        in_=wdns[mo][:, kg * 16:(kg + 1) * 16, :])
                             .then_inc(p.s_wld, 16),
                            (e.dma_start(out=lwf[:], in_=bdnl[mo])
                             .then_inc(p.s_wld, 16) if kg == 0 else
                             e.sem_inc(p.s_wld, 16))))
                        pev = p.ct["pe"]
                        plv = p.inc("pool")
                        p.add("gpsimd", lambda e, wldv=wldv, pev=pev: (
                            W(p, e, ("wld", wldv), ("pe", pev)),
                            e.tensor_copy(out=wfc[:], in_=wqi[:]).then_inc(p.s_pool, 1)))
                        dv = p.inc("dve")
                        p.add("vector", lambda e, plv=plv, kg=kg: (
                            W(p, e, ("pool", plv)),
                            e.tensor_scalar(out=wfc[:], in0=wfc[:], scalar1=7.5,
                                            scalar2=None, op0=ALU.subtract),
                            e.tensor_mul(out=wbufb[:], in0=wfc[:], in1=wsc[:]),
                            (e.tensor_copy(out=lwb[:], in_=lwf[:])
                             if kg == 0 else e.engine_nop())
                             .then_inc(p.s_dve, 1)))
                        pv = p.inc("pe")
                        def dnmm(e, dv=dv, kg=kg, t2v=t2_dve):
                            W(p, e, ("dve", dv), ("dve", t2v))
                            last = None
                            for tt in range(2):
                                for kk in range(16):
                                    last = e.matmul(
                                        p_d[:, tt * 512:(tt + 1) * 512],
                                        wbufb[:, kk * 128:(kk + 1) * 128],
                                        x2[:, (kg * 16 + kk) * T + tt * 512:
                                           (kg * 16 + kk) * T + tt * 512 + 512],
                                        start=(kg == 0 and kk == 0), stop=False)
                                if kg == 3:
                                    last = e.matmul(
                                        p_d[:, tt * 512:(tt + 1) * 512],
                                        lwb[:, :], t2b[:, tt * 512:(tt + 1) * 512],
                                        start=False, stop=True)
                            last.then_inc(p.s_pe, 1)
                        p.add("tensor", dnmm)
                        pvs.append(pv)
                    xldv = p.inc("ld", 16)
                    p.add("sync", lambda e, mo=mo, dvw=p.ct["dve"]: (
                        W(p, e, ("dve", dvw)),
                        e.dma_start(out=xck[:], in_=xs[:, mo * T:(mo + 1) * T])
                         .then_inc(p.s_ld, 16)))
                    dv = p.inc("dve")
                    p.add("vector", lambda e, pv=pvs[-1], mo=mo, xv=xldv: (
                        W(p, e, ("pe", pv), ("ld", xv)),
                        e.tensor_scalar(out=ybuf[:], in0=p_d[:], scalar1=sbdn[:, mo:mo + 1],
                                        scalar2=None, op0=ALU.add),
                        e.tensor_add(out=ybuf[:], in0=ybuf[:], in1=xck[:])
                         .then_inc(p.s_dve, 1)))
                    stv = p.inc("sto", 16)
                    p.add("gpsimd", lambda e, dv=dv, mo=mo: (
                        W(p, e, ("dve", dv)),
                        e.dma_start(out=xout[mo * 128:(mo + 1) * 128, :],
                                    in_=ybuf[:]).then_inc(p.s_sto, 16)))

        p.add("sync", lambda e, v=p.ct["sto"]: W(p, e, ("sto", v)))
        p.build(block)
    return nc


def stage_d(inputs, core, xo_outs, cfm_outs):
    s, half = core // 2, core % 2
    xT = np.asarray(xo_outs[core])
    a0 = np.asarray(cfm_outs[2 * s])[:, half * 1024:(half + 1) * 1024]
    a1 = np.asarray(cfm_outs[2 * s + 1])[:, half * 1024:(half + 1) * 1024]
    ofm = np.ascontiguousarray(np.concatenate([a0, a1], 0))

    wo = np.asarray(inputs["L1_attn_wo"])
    wq = np.asarray(inputs["L1_ffn_w_up_q"])
    sup = np.asarray(inputs["L1_ffn_w_up_scale"])
    sup_exp = np.repeat(sup, 64, axis=1)
    wdq = np.asarray(inputs["L1_ffn_w_down_q"])
    sdn = np.asarray(inputs["L1_ffn_w_down_scale"])
    sdn_exp = np.repeat(sdn, 64, axis=1)
    return dict(
        xT=xT, ofm=ofm,
        woT=_tile_lhsT(np.ascontiguousarray(wo.T)),
        wupq=_tile_lhsT(np.ascontiguousarray(wq.T)),
        wups=_tile_lhsT(np.ascontiguousarray(sup_exp.T)),
        bup=np.ascontiguousarray(
            np.asarray(inputs["L1_ffn_b_up"]).reshape(64, 128).T),
        aup=_tile_lhsT(np.asarray(inputs["L1_ffn_w_up_lora_a"]), mm=16)
            .reshape(128, 256),
        bupl=np.ascontiguousarray(
            np.asarray(inputs["L1_ffn_w_up_lora_b"]).reshape(16, 64, 128)
            .transpose(1, 0, 2)),
        wdnq=_tile_lhsT(np.ascontiguousarray(wdq.T)),
        wdns=_tile_lhsT(np.ascontiguousarray(sdn_exp.T)),
        bdn=np.ascontiguousarray(
            np.asarray(inputs["L1_ffn_b_down"]).reshape(16, 128).T),
        adn=_tile_lhsT(np.asarray(inputs["L1_ffn_w_down_lora_a"]), mm=16)
            .reshape(128, 1024),
        bdnl=np.ascontiguousarray(
            np.asarray(inputs["L1_ffn_w_down_lora_b"]).reshape(16, 16, 128)
            .transpose(1, 0, 2)),
    )


def run_launch_d(inputs, xo_outs, cfm_outs):
    nc = _get("d", build_launch_d)
    maps = [stage_d(inputs, c, xo_outs, cfm_outs) for c in range(NCORES)]
    res = run_bass_kernel_spmd(nc, maps, core_ids=list(range(NCORES)))
    return [np.asarray(r["xout"]) for r in res.results]


_NC_CACHE = {}


def _get(name, builder):
    if name not in _NC_CACHE:
        _NC_CACHE[name] = builder()
    return _NC_CACHE[name]


def run_launch_a(inputs):
    nc = _get("a", build_launch_a)
    maps = [stage_a(inputs, c) for c in range(NCORES)]
    res = run_bass_kernel_spmd(nc, maps, core_ids=list(range(NCORES)))
    return [np.asarray(r["afm"]) for r in res.results]


def kernel(**inputs):
    afm = run_launch_a(inputs)
    xo = run_launch_b(inputs, afm)
    cfm = run_launch_c(inputs, xo)
    xout = run_launch_d(inputs, xo, cfm)
    out = np.empty((4, S, D), np.float32)
    for c in range(NCORES):
        s, half = c // 2, c % 2
        out[s, half * 1024:(half + 1) * 1024, :] = np.asarray(xout[c]).T
    return out
